# version 08
# baseline (speedup 1.0000x reference)
import numpy as np
import ml_dtypes
import concourse.bass as bass
import concourse.mybir as mybir
from concourse.bass_utils import run_bass_kernel_spmd

F32 = mybir.dt.float32
BF16 = mybir.dt.bfloat16
I32 = mybir.dt.int32
AF = mybir.ActivationFunctionType
ALU = mybir.AluOpType
AX = mybir.AxisListType

D = 2048
T = 2048
EPS = 1e-5
KB = 1024
SB0 = 16512

OFF_X = SB0
OFF_L = SB0 + 64 * KB
OFF_KQ = SB0 + 128 * KB
OFF_TAB = SB0 + 160 * KB
OFF_WB = SB0 + 176 * KB
OFF_WD = SB0 + 184 * KB
OFF_VW = SB0 + 188 * KB
OFF_F = SB0 + 192 * KB
OFF_G = SB0 + 200 * KB
OFF_TC = SB0 + 204 * KB


def build_program(debug=False):
    nc = bass.Bass("TRN2", target_bir_lowering=False)

    x_h = nc.dram_tensor("x", [T, D], F32, kind="ExternalInput").ap()
    wqkv_h = nc.dram_tensor("wqkv", [128, 48, 16, 128], F32, kind="ExternalInput").ap()
    wout_h = nc.dram_tensor("wout", [128, 16, 16, 128], F32, kind="ExternalInput").ap()
    wg_h = nc.dram_tensor("wg", [128, 64, 16, 128], I32, kind="ExternalInput").ap()
    wu_h = nc.dram_tensor("wu", [128, 64, 16, 128], I32, kind="ExternalInput").ap()
    wd_h = nc.dram_tensor("wd", [128, 64, 2, 8, 128], I32, kind="ExternalInput").ap()
    bqkv_h = nc.dram_tensor("bqkv", [128, 48], F32, kind="ExternalInput").ap()
    bo_h = nc.dram_tensor("bo", [128, 16], F32, kind="ExternalInput").ap()
    gs_h = nc.dram_tensor("gs", [128, 64], F32, kind="ExternalInput").ap()
    us_h = nc.dram_tensor("us", [128, 64], F32, kind="ExternalInput").ap()
    ds_h = nc.dram_tensor("ds", [128, 16], F32, kind="ExternalInput").ap()
    cos_h = nc.dram_tensor("cost", [128, 4, 2048], BF16, kind="ExternalInput").ap()
    sin_h = nc.dram_tensor("sint", [128, 4, 2048], BF16, kind="ExternalInput").ap()
    idn_h = nc.dram_tensor("idn", [128, 128], F32, kind="ExternalInput").ap()
    y_h = nc.dram_tensor("y", [T, D], F32, kind="ExternalOutput").ap()

    a_in = nc.dram_tensor("a_in", [128, 16], F32)
    a_out = nc.dram_tensor("a_out", [128, 16], F32)

    dbg = {}
    if debug:
        for nm, shp, dt in [("d_xf", [128, 32768], BF16), ("d_ln1", [128, 32768], BF16),
                            ("d_w", [1, 1024], F32), ("d_apar", [128, 16], F32),
                            ("d_afull", [128, 16], F32), ("d_kq", [128, 16384], BF16),
                            ("d_L", [128, 32768], BF16), ("d_h", [128, 32768], BF16),
                            ("d_ln3", [128, 8192], BF16), ("d_ffn", [128, 8192], BF16),
                            ("d_wbc", [128, 1024], F32), ("d_vw", [128, 1024], BF16),
                            ("d_kp", [128, 1024], BF16), ("d_kv0", [128, 1024], BF16),
                            ("d_ap0", [128, 16], F32), ("d_cos", [128, 4, 1024], BF16)]:
            dbg[nm] = nc.dram_tensor(nm, shp, dt, kind="ExternalOutput").ap()

    AL = nc.alloc_sbuf_tensor_at
    Xb = AL("Xb", [128, 32768], BF16, offset=OFF_X).ap()
    Lb = AL("Lb", [128, 32768], BF16, offset=OFF_L).ap()
    XT = AL("XT", [128, 2, 2048], F32, offset=OFF_L).ap()
    SQd = AL("SQd", [128, 2048], F32, offset=OFF_L + 16 * KB).ap()
    KQ = AL("KQ", [128, 16384], BF16, offset=OFF_KQ).ap()
    r1row = AL("r1row", [1, 2048], F32, offset=OFF_KQ).ap()
    mr1row = AL("mr1row", [1, 2048], F32, offset=OFF_KQ + 8 * KB).ap()
    st1 = AL("st1", [16, 128], F32, offset=OFF_KQ + 16 * KB).ap()
    st2 = AL("st2", [16, 128], F32, offset=OFF_KQ + 16 * KB + 512).ap()
    BC1 = AL("BC1", [128, 2048], BF16, offset=OFF_KQ + 16 * KB).ap()
    BC2 = AL("BC2", [128, 2048], BF16, offset=OFF_KQ + 24 * KB).ap()
    ln3f = AL("ln3f", [128, 8192], BF16, offset=OFF_KQ).ap()
    ffnf = AL("ffnf", [128, 8192], BF16, offset=OFF_KQ + 16 * KB).ap()
    outTok = AL("outTok", [128, 2, 2048], F32, offset=OFF_KQ).ap()
    outTok4 = AL("outTok4", [128, 4, 2048], F32, offset=OFF_TAB).ap()
    COSH = AL("COSH", [128, 4, 1024], BF16, offset=OFF_TAB).ap()
    SINH = AL("SINH", [128, 4, 1024], BF16, offset=OFF_TAB + 8 * KB).ap()
    WU2 = AL("WU2", [128, 2, 16, 128], BF16, offset=OFF_TAB).ap()
    shrow = AL("shrow", [1, 512], F32, offset=OFF_F + 1 * KB).ap()
    GB = AL("GB", [128, 2, 2, 16, 128], BF16, offset=OFF_TAB).ap()
    UB = AL("UB", [128, 2, 2, 16, 128], BF16, offset=OFF_WB).ap()
    DNB = AL("DNB", [128, 2, 16, 4, 128], BF16, offset=OFF_TAB).ap()
    WB = AL("WB", [128, 2, 16, 128], BF16, offset=OFF_WB).ap()
    s2row = AL("s2row", [1, 1024], F32, offset=OFF_VW).ap()
    q2row = AL("q2row", [1, 1024], F32, offset=OFF_G).ap()
    WD2 = AL("WD2", [128, 2, 4, 128], BF16, offset=OFF_WD).ap()
    s34row = AL("s34row", [1, 512], F32, offset=OFF_VW).ap()
    q34row = AL("q34row", [1, 512], F32, offset=OFF_VW + 2 * KB).ap()
    VW = AL("VW", [128, 2, 1024], BF16, offset=OFF_VW).ap()
    wrow = AL("wrow", [1, 1024], F32, offset=OFF_VW).ap()
    WBC = AL("WBC", [128, 1024], F32, offset=OFF_F).ap()
    r2b = AL("r2b", [128, 1024], BF16, offset=OFF_F).ap()
    m2r2b = AL("m2r2b", [128, 1024], BF16, offset=OFF_F + 4 * KB).ap()
    r3b = AL("r3b", [128, 512], BF16, offset=OFF_F).ap()
    m3r3b = AL("m3r3b", [128, 512], BF16, offset=OFF_F + 2 * KB).ap()
    r4b = AL("r4b", [128, 512], BF16, offset=OFF_F + 4 * KB).ap()
    m4r4b = AL("m4r4b", [128, 512], BF16, offset=OFF_F + 6 * KB).ap()
    GSQ = AL("GSQ", [128, 2048], BF16, offset=OFF_G).ap()
    G16a = AL("G16a", [128, 1024], BF16, offset=OFF_G).ap()
    G16b = AL("G16b", [128, 1024], BF16, offset=OFF_G + 2 * KB).ap()
    Gf = AL("Gf", [128, 1024], F32, offset=OFF_G).ap()
    Gf5a = AL("Gf5a", [128, 512], F32, offset=OFF_G).ap()
    Gf5b = AL("Gf5b", [128, 512], F32, offset=OFF_G + 2 * KB).ap()

    t0 = OFF_TC
    s1c = AL("s1c", [128, 16], F32, offset=t0).ap()
    sqc = AL("sqc", [128, 16], F32, offset=t0 + 64).ap()
    m1c = AL("m1c", [128, 16], F32, offset=t0 + 128).ap()
    r1c = AL("r1c", [128, 16], F32, offset=t0 + 192).ap()
    mr1c = AL("mr1c", [128, 16], F32, offset=t0 + 256).ap()
    bqkv = AL("bqkvs", [128, 48], F32, offset=t0 + 320).ap()
    bo = AL("bos", [128, 16], F32, offset=t0 + 512).ap()
    gsc = AL("gscs", [128, 64], F32, offset=t0 + 576).ap()
    usc = AL("uscs", [128, 64], F32, offset=t0 + 832).ap()
    dsc = AL("dscs", [128, 16], F32, offset=t0 + 1088).ap()
    Apar = AL("Apar", [128, 16], F32, offset=t0 + 1152).ap()
    Afull = AL("Afull", [128, 16], F32, offset=t0 + 1216).ap()
    bA = AL("bAs", [128, 16], F32, offset=t0 + 1280).ap()
    rcol = AL("rcol", [128, 1], F32, offset=t0 + 1344).ap()
    onesb = AL("onesb", [128, 1], BF16, offset=t0 + 1376).ap()
    onesf = AL("onesf", [1, 128], F32, offset=t0 + 1408).ap()
    ident = AL("idents", [128, 128], F32, offset=t0 + 1920).ap()

    pM0 = nc.alloc_psum_tensor("pM0", [128, 1024], F32).ap()
    pM1 = nc.alloc_psum_tensor("pM1", [128, 1024], F32).ap()
    pT = nc.alloc_psum_tensor("pT", [128, 1024], F32).ap()
    pR = nc.alloc_psum_tensor("pR", [128, 1024], F32).ap()
    pM = [pM0, pM1]

    import contextlib
    from collections import defaultdict
    es = contextlib.ExitStack()
    S = {}
    for nm in ("x0", "x1", "tb", "ct", "wb0", "wb1", "wu0", "wu1", "wd0", "wd1",
               "gr", "cc", "gc", "pe", "ve", "se", "so"):
        S[nm] = es.enter_context(nc.semaphore(nm))
    cnt = defaultdict(int)
    q = {"sy": [], "gp": [], "pe": [], "ve": [], "se": []}
    marks = []

    def mark(name):
        marks.append((name, {e: len(q[e]) for e in q}))

    def emit(eng, fn):
        q[eng].append(fn)

    def wait(eng, sem, val):
        if val > 0:
            emit(eng, lambda E, s=S[sem], v=val: E.wait_ge(s, v))

    def dma(engq, sem, out, in_):
        if cnt[sem] > 0:
            wait(engq, sem, cnt[sem])
        cnt[sem] += 16
        emit(engq, lambda E, o=out, i=in_, s=S[sem]:
             E.dma_start(out=o, in_=i).then_inc(s, 16))
        return (sem, cnt[sem])

    def waitt(eng, tok):
        if tok:
            wait(eng, tok[0], tok[1])

    def vop(fn):
        wait("ve", "ve", cnt["ve"])
        cnt["ve"] += 1
        emit("ve", lambda E, f=fn, s=S["ve"]: f(E).then_inc(s, 1))
        return cnt["ve"]

    def sop(fn):
        wait("se", "se", cnt["se"])
        cnt["se"] += 1
        emit("se", lambda E, f=fn, s=S["se"]: f(E).then_inc(s, 1))
        return cnt["se"]

    def gop(fn):
        wait("gp", "gc", cnt["gc"])
        cnt["gc"] += 1
        emit("gp", lambda E, f=fn, s=S["gc"]: f(E).then_inc(s, 1))
        return cnt["gc"]

    def peg(fns):
        for f in fns[:-1]:
            emit("pe", f)
        cnt["pe"] += 1
        emit("pe", lambda E, f=fns[-1], s=S["pe"]: f(E).then_inc(s, 1))
        return cnt["pe"]

    def aux_guard():
        wait("pe", "ve", cnt["ve"])
        wait("pe", "se", cnt["se"])

    def fsl(buf, c, off, n):
        return buf[:, 2048 * c + off: 2048 * c + off + n]

    def kqsl(c, off, n):
        return KQ[:, 1024 * c + off: 1024 * c + off + n]

    for out, in_ in [(bqkv, bqkv_h), (bo, bo_h), (gsc, gs_h), (usc, us_h),
                     (dsc, ds_h), (ident, idn_h)]:
        v_const = dma("sy", "ct", out, in_)
    waitt("pe", v_const)
    waitt("ve", v_const)
    waitt("se", v_const)
    vop(lambda E: E.memset(onesb[:], 1.0))
    vop(lambda E: E.memset(onesf[:], 1.0))
    vop(lambda E: E.memset(Apar[:], 0.0))

    mark("P1")
    ev_batch = 0
    cons = [(0, 0, 0)] * 16
    for tc in range(16):
        if tc >= 2:
            p, s, v = cons[tc - 2]
            wait("sy", "pe", p)
            wait("sy", "se", s)
            wait("sy", "ve", v)
        xv = dma("sy", "x%d" % (tc % 2), XT[:, tc % 2, :],
                 x_h[128 * tc:128 * (tc + 1), :])
        waitt("se", xv)
        se_sq = sop(lambda E, tc=tc: E.activation(
            SQd[:], XT[:, tc % 2, :], AF.Square, accum_out=sqc[:, tc:tc + 1]))
        waitt("ve", xv)
        ve_rs = vop(lambda E, tc=tc: E.reduce_sum(
            s1c[:, tc:tc + 1], XT[:, tc % 2, :], axis=AX.X))
        pe_last = 0
        for b in range(2):
            waitt("pe", xv)
            aux_guard()
            fns = []
            for j in range(8):
                c = 8 * b + j
                fns.append(lambda E, tc=tc, c=c, j=j: E.transpose(
                    pT[:, 128 * j:128 * (j + 1)],
                    XT[:, tc % 2, 128 * c:128 * (c + 1)], ident))
            pe_last = peg(fns)
            wait("ve", "pe", pe_last)
            dst = bass.AP(Xb.tensor, 2048 * 8 * b + 128 * tc,
                          [[32768, 128], [2048, 8], [1, 128]])
            src = bass.AP(pT.tensor, 0, [[1024, 128], [128, 8], [1, 128]])
            ev_batch = vop(lambda E, d=dst, s=src: E.tensor_copy(d, s))
        cons[tc] = (pe_last, se_sq, max(ve_rs, ev_batch))

    wait("ve", "se", cnt["se"])
    vop(lambda E: E.tensor_scalar_mul(m1c[:], s1c[:], 1.0 / D))
    vop(lambda E: E.tensor_scalar_mul(r1c[:], sqc[:], 1.0 / D))
    vop(lambda E: E.tensor_mul(mr1c[:], m1c[:], m1c[:]))
    vop(lambda E: E.tensor_sub(r1c[:], r1c[:], mr1c[:]))
    v_var = vop(lambda E: E.tensor_scalar_add(r1c[:], r1c[:], EPS))
    wait("se", "ve", v_var)
    s_sd = sop(lambda E: E.activation(r1c[:], r1c[:], AF.Sqrt))
    wait("ve", "se", s_sd)
    vop(lambda E: E.reciprocal(r1c[:], r1c[:]))
    v_r1 = vop(lambda E: E.tensor_mul(mr1c[:], m1c[:], r1c[:]))
    wait("pe", "ve", v_r1)
    aux_guard()
    p_st = peg([lambda E: E.transpose(pT[0:16, 0:128], r1c[:], ident),
                lambda E: E.transpose(pT[0:16, 128:256], mr1c[:], ident)])
    wait("ve", "pe", p_st)
    vop(lambda E: E.tensor_copy(st1[:], pT[0:16, 0:128]))
    v_st = vop(lambda E: E.tensor_copy(st2[:], pT[0:16, 128:256]))
    wait("gp", "ve", v_st)
    v_row1 = dma("gp", "gr",
                 bass.AP(r1row.tensor, 0, [[2048, 1], [128, 16], [1, 128]]), st1[:])
    v_row2 = dma("gp", "gr",
                 bass.AP(mr1row.tensor, 0, [[2048, 1], [128, 16], [1, 128]]), st2[:])

    def bcast(row, dst, width, ready):
        vv = 0
        pos = 0
        while pos < width:
            w = min(1024, width - pos)
            wait("pe", ready[0], ready[1])
            aux_guard()
            fns = []
            for n in range(0, w, 512):
                nn = min(512, w - n)
                fns.append(lambda E, p=pos, n=n, nn=nn: E.matmul(
                    pT[:, n:n + nn], onesf[:],
                    bass.AP(row.tensor, row.offset + p + n, [[width, 1], [1, nn]]),
                    start=True, stop=True))
            pv = peg(fns)
            wait("ve", "pe", pv)
            vv = vop(lambda E, p=pos, w=w, d=dst: E.tensor_copy(d[:, p:p + w], pT[:, 0:w]))
            pos += w
        return vv

    waitt("ve", v_row2)
    v_bc1 = bcast(r1row, BC1, 2048, v_row1)
    v_bc2 = bcast(mr1row, BC2, 2048, v_row2)

    mark("P2")
    v_ln1 = 0
    for c in range(16):
        vop(lambda E, c=c: E.tensor_mul(fsl(Lb, c, 0, 2048), fsl(Xb, c, 0, 2048), BC1[:]))
        v_ln1 = vop(lambda E, c=c: E.tensor_sub(fsl(Lb, c, 0, 2048),
                                                fsl(Lb, c, 0, 2048), BC2[:]))
    if debug:
        wait("se", "ve", v_ln1)
        dma("se", "so", dbg["d_xf"], Xb[:])
        dma("se", "so", dbg["d_ln1"], Lb[:])

    mm = {"evac": [0, 0], "par": 0}

    def mm_group(wb_t, wb_slot, wtok, moving_fn, extra=None):
        par = mm["par"]
        mm["par"] ^= 1
        buf = pM[par]
        wait("pe", wtok[0], wtok[1])
        if mm["evac"][par]:
            wait("pe", mm["evac"][par][0], mm["evac"][par][1])
        if extra is not None:
            wait("pe", extra[0], extra[1])
        fns = []
        for n in (0, 512):
            for dc in range(16):
                fns.append(lambda E, dc=dc, t=wb_t, s=wb_slot: E.ldweights(t[:, s, dc, :]))
                fns.append(lambda E, dc=dc, n=n, b=buf, t=wb_t, s=wb_slot: E.matmul(
                    b[:, n:n + 512], t[:, s, dc, :], moving_fn(dc, n),
                    start=(dc == 0), stop=(dc == 15)))
        return par, buf, peg(fns)

    def set_evac(par, v):
        mm["evac"][par] = v

    v_apar = 0
    last_pe = 0
    v_tab_use = 0
    for h in range(2):
        mark("P3h%d" % h)
        hoff = 1024 * h
        wait("ve", "ve", max(v_bc1, v_bc2, v_ln1))
        wait("sy", "ve", v_tab_use)
        dma("sy", "tb", COSH[:], cos_h[:, :, hoff:hoff + 1024])
        v_tab = dma("sy", "tb", SINH[:], sin_h[:, :, hoff:hoff + 1024])
        pe_grp = [0] * 16
        pe_sig = 0
        for c in range(16):
            g = 16 + c
            slot = c % 2
            after = pe_grp[c - 2] if c >= 2 else last_pe
            wait("gp", "pe", after)
            wv = dma("gp", "wb%d" % slot, WB[:, slot, :, :], wqkv_h[:, g])
            par, buf, pv = mm_group(WB, slot, wv,
                                    lambda dc, n, hoff=hoff: fsl(Lb, dc, hoff + n, 512),
                                    extra=("ve", v_ln1) if (h == 0 and c == 0) else None)
            pe_grp[c] = pv
            wait("se", "pe", pv)
            sv = sop(lambda E, c=c, g=g, b=buf: E.activation(
                kqsl(c, 0, 1024), b[:], AF.Identity, bias=bqkv[:, g:g + 1], scale=1.0))
            set_evac(par, ("se", sv))
            if c % 2 == 1:
                wait("ve", "pe", pe_sig)
                wait("ve", "se", sv)
                ksrc = bass.AP(KQ.tensor, 1024 * (c - 1),
                               [[16384, 128], [1024, 2], [1, 1024]])
                kdst = bass.AP(GSQ.tensor, 0, [[2048, 128], [1024, 2], [1, 1024]])
                vsq = vop(lambda E, a=kdst, b=ksrc: E.tensor_mul(a, b, b))
                wait("pe", "ve", vsq)
                aux_guard()
                fns = [lambda E: E.ldweights(onesb[:])]
                for cc in (c - 1, c):
                    o = 1024 * (cc - c + 1)
                    fns.append(lambda E, cc=cc, o=o: E.matmul(
                        pR[0:1, 0:512], onesb[:], GSQ[:, o:o + 512],
                        start=(cc == 0), stop=(cc == 15)))
                    fns.append(lambda E, cc=cc, o=o: E.matmul(
                        pR[0:1, 512:1024], onesb[:], GSQ[:, o + 512:o + 1024],
                        start=(cc == 0), stop=(cc == 15)))
                pe_sig = peg(fns)
        wait("se", "pe", pe_sig)
        ssd = sop(lambda E: E.activation(pR[0:1, 0:1024], pR[0:1, 0:1024], AF.Sqrt))
        wait("ve", "se", ssd)
        vop(lambda E: E.tensor_scalar_max(pR[0:1, 0:1024], pR[0:1, 0:1024], 1e-12))
        v_w = vop(lambda E: E.reciprocal(wrow[:], pR[0:1, 0:1024]))
        if debug and h == 0:
            wait("se", "ve", v_w)
            tdw = dma("se", "so", dbg["d_w"], wrow[:])
            waitt("ve", tdw)
        v_wb = bcast(wrow, WBC, 1024, ("ve", v_w))
        for c in range(16):
            g = 32 + c
            slot = c % 2
            after = pe_grp[c - 2] if c >= 2 else pe_grp[15]
            wait("gp", "pe", after)
            wv = dma("gp", "wb%d" % slot, WB[:, slot, :, :], wqkv_h[:, g])
            par, buf, pv = mm_group(WB, slot, wv,
                                    lambda dc, n, hoff=hoff: fsl(Lb, dc, hoff + n, 512))
            pe_grp[c] = pv
            vpar = c % 2
            wait("ve", "pe", pv)
            wait("ve", "ve", v_wb)
            vv = vop(lambda E, g=g, b=buf, vp=vpar: E.scalar_tensor_tensor(
                VW[:, vp, :], b[:], bqkv[:, g:g + 1], WBC[:], ALU.add, ALU.mult))
            set_evac(par, ("ve", vv))
            if debug and h == 0 and c == 0:
                wait("se", "ve", vv)
                t1 = dma("se", "so", dbg["d_wbc"], WBC[:])
                t2 = dma("se", "so", dbg["d_vw"], VW[:, 0, :])
                waitt("ve", t2)
            if c == 0:
                waitt("ve", v_tab)
            if c < 4:
                vop(lambda E, c=c: E.tensor_mul(
                    G16a[:], kqsl(c, 0, 1024), COSH[:, c, :]))
                vop(lambda E, c=c: E.tensor_mul(
                    G16b[:], kqsl(c + 4, 0, 1024), SINH[:, c, :]))
                vop(lambda E: E.tensor_sub(G16a[:], G16a[:], G16b[:]))
                vop(lambda E, vp=vpar: E.tensor_mul(G16b[:], G16a[:], VW[:, vp, :]))
            elif c < 8:
                vop(lambda E, c=c: E.tensor_mul(
                    G16a[:], kqsl(c, 0, 1024), COSH[:, c - 4, :]))
                vop(lambda E, c=c: E.tensor_mul(
                    G16b[:], kqsl(c - 4, 0, 1024), SINH[:, c - 4, :]))
                vop(lambda E: E.tensor_add(G16a[:], G16a[:], G16b[:]))
                vop(lambda E, vp=vpar: E.tensor_mul(G16b[:], G16a[:], VW[:, vp, :]))
            else:
                vop(lambda E, c=c, vp=vpar: E.tensor_mul(
                    G16b[:], kqsl(c, 0, 1024), VW[:, vp, :]))
            if debug and h == 0 and c == 0:
                wait("se", "ve", cnt["ve"])
                dma("se", "so", dbg["d_cos"], COSH[:])
                dma("se", "so", dbg["d_kp"], G16a[:])
                t3 = dma("se", "so", dbg["d_kv0"], G16b[:])
                waitt("ve", t3)
            vop(lambda E: E.reduce_sum(rcol[:], G16b[:], axis=AX.X))
            v_apar = vop(lambda E, c=c: E.tensor_add(
                Apar[:, c:c + 1], Apar[:, c:c + 1], rcol[:]))
            v_tab_use = v_apar
            if debug and h == 0 and c == 0:
                wait("se", "ve", v_apar)
                t4 = dma("se", "so", dbg["d_ap0"], Apar[:])
                waitt("ve", t4)
        last_pe = pe_grp[15]

    if debug:
        wait("se", "ve", v_apar)
        dma("se", "so", dbg["d_apar"], Apar[:])

    mark("P4")
    wait("gp", "ve", v_apar)
    v_gb = dma("gp", "gr", a_in.ap(), Apar[:])
    waitt("gp", v_gb)
    cnt["cc"] += 1
    emit("gp", lambda E: E.collective_compute(
        "AllReduce", ALU.add, replica_groups=[[0, 1], [2, 3], [4, 5], [6, 7]],
        ins=[a_in.ap().opt()], outs=[a_out.ap().opt()]).then_inc(S["cc"], 1))
    wait("gp", "cc", cnt["cc"])
    v_afull = dma("gp", "gr", Afull[:], a_out.ap())
    waitt("gp", v_afull)
    if debug:
        waitt("se", v_afull)
        dma("se", "so", dbg["d_afull"], Afull[:])
    waitt("ve", v_afull)
    v_bA = vop(lambda E: E.tensor_mul(bA[:], bqkv[:, 0:16], Afull[:]))

    v_x2 = 0
    for h in range(2):
        mark("P5h%d" % h)
        hoff = 1024 * h
        wait("sy", "ve", v_tab_use)
        dma("sy", "tb", COSH[:], cos_h[:, :, hoff:hoff + 1024])
        v_tab = dma("sy", "tb", SINH[:], sin_h[:, :, hoff:hoff + 1024])
        pe_grp = [0] * 16
        v_qe = 0
        for c in range(16):
            slot = c % 2
            after = pe_grp[c - 2] if c >= 2 else last_pe
            wait("gp", "pe", after)
            wv = dma("gp", "wb%d" % slot, WB[:, slot, :, :], wqkv_h[:, c])
            par, buf, pv = mm_group(WB, slot, wv,
                                    lambda dc, n, hoff=hoff: fsl(Lb, dc, hoff + n, 512))
            pe_grp[c] = pv
            wait("se", "pe", pv)
            if c >= 8:
                wait("se", "ve", v_bA)
                sv = sop(lambda E, c=c, b=buf: E.activation(
                    kqsl(c, 0, 1024), b[:], AF.Identity, bias=bA[:, c:c + 1],
                    scale=Afull[:, c:c + 1]))
            else:
                sv = sop(lambda E, c=c, b=buf: E.activation(
                    kqsl(c, 0, 1024), b[:], AF.Identity, bias=bqkv[:, c:c + 1],
                    scale=1.0))
            set_evac(par, ("se", sv))
            v_qe = sv
        last_pe = pe_grp[15]
        wait("ve", "se", v_qe)
        waitt("ve", v_tab)
        v_y = 0
        for j in range(4):
            vop(lambda E, j=j: E.scalar_tensor_tensor(
                G16a[:], kqsl(j, 0, 1024), Afull[:, j:j + 1],
                COSH[:, j, :], ALU.mult, ALU.mult))
            vop(lambda E, j=j: E.scalar_tensor_tensor(
                G16b[:], kqsl(j + 4, 0, 1024), Afull[:, j:j + 1],
                SINH[:, j, :], ALU.mult, ALU.mult))
            vop(lambda E, j=j: E.scalar_tensor_tensor(
                VW[:, 0, :], kqsl(j + 4, 0, 1024), Afull[:, j + 4:j + 5],
                COSH[:, j, :], ALU.mult, ALU.mult))
            vop(lambda E, j=j: E.scalar_tensor_tensor(
                VW[:, 1, :], kqsl(j, 0, 1024), Afull[:, j + 4:j + 5],
                SINH[:, j, :], ALU.mult, ALU.mult))
            vop(lambda E, j=j: E.tensor_sub(kqsl(j, 0, 1024), G16a[:], G16b[:]))
            v_y = vop(lambda E, j=j: E.tensor_add(kqsl(j + 4, 0, 1024),
                                                  VW[:, 0, :], VW[:, 1, :]))
        v_tab_use = v_y
        if debug and h == 0:
            wait("se", "ve", v_y)
            tkq = dma("se", "so", dbg["d_kq"], KQ[:])
            waitt("se", tkq)
        mark("P6h%d" % h)
        pe_s2 = 0
        v_at = 0
        for g in range(16):
            slot = g % 2
            after = pe_grp[g - 2] if g >= 2 else last_pe
            wait("gp", "pe", after)
            wv = dma("gp", "wb%d" % slot, WB[:, slot, :, :], wout_h[:, g])
            par, buf, pv = mm_group(WB, slot, wv,
                                    lambda dc, n: kqsl(dc, n, 512),
                                    extra=("ve", v_y) if g == 0 else None)
            pe_grp[g] = pv
            wait("ve", "pe", pv)
            v_at = vop(lambda E, g=g, b=buf, hoff=hoff: E.scalar_tensor_tensor(
                fsl(Lb, g, hoff, 1024), b[:], bo[:, g:g + 1],
                fsl(Lb, g, hoff, 1024), ALU.add, ALU.add))
            set_evac(par, ("ve", v_at))
            wait("ve", "pe", pe_s2)
            vsq = vop(lambda E, g=g, hoff=hoff: E.tensor_mul(
                G16a[:], fsl(Lb, g, hoff, 1024), fsl(Lb, g, hoff, 1024)))
            wait("pe", "ve", vsq)
            aux_guard()
            pe_s2 = peg([lambda E: E.ldweights(onesb[:]),
                         lambda E, g=g, hoff=hoff: E.matmul(
                             pR[0:1, 0:512], onesb[:], fsl(Lb, g, hoff, 512),
                             start=(g == 0), stop=(g == 15)),
                         lambda E, g=g, hoff=hoff: E.matmul(
                             pR[0:1, 512:1024], onesb[:], fsl(Lb, g, hoff + 512, 512),
                             start=(g == 0), stop=(g == 15)),
                         lambda E, g=g: E.matmul(
                             pT[0:1, 0:512], onesb[:], G16a[:, 0:512],
                             start=(g == 0), stop=(g == 15)),
                         lambda E, g=g: E.matmul(
                             pT[0:1, 512:1024], onesb[:], G16a[:, 512:1024],
                             start=(g == 0), stop=(g == 15))])
        last_pe = pe_grp[15]
        wait("ve", "pe", pe_s2)
        vop(lambda E: E.tensor_copy(s2row[:], pR[0:1, 0:1024]))
        vop(lambda E: E.tensor_copy(q2row[:], pT[0:1, 0:1024]))
        vop(lambda E: E.tensor_scalar_mul(s2row[:], s2row[:], 1.0 / D))
        vop(lambda E: E.tensor_scalar_mul(q2row[:], q2row[:], 1.0 / D))
        vop(lambda E: E.tensor_mul(pT[0:1, 0:1024], s2row[:], s2row[:]))
        vop(lambda E: E.tensor_sub(q2row[:], q2row[:], pT[0:1, 0:1024]))
        v_v2 = vop(lambda E: E.tensor_scalar_add(q2row[:], q2row[:], EPS))
        wait("se", "ve", v_v2)
        s_r2 = sop(lambda E: E.activation(q2row[:], q2row[:], AF.Sqrt))
        wait("ve", "se", s_r2)
        v_r2 = vop(lambda E: E.reciprocal(q2row[:], q2row[:]))
        v_mr2 = vop(lambda E: E.tensor_mul(s2row[:], s2row[:], q2row[:]))
        v_b1 = bcast(q2row, r2b, 1024, ("ve", v_r2))
        v_b2 = bcast(s2row, m2r2b, 1024, ("ve", v_mr2))
        wait("ve", "ve", max(v_b1, v_b2))
        for c in range(16):
            vop(lambda E, c=c, hoff=hoff: E.tensor_mul(
                G16a[:], fsl(Lb, c, hoff, 1024), r2b[:]))
            vop(lambda E: E.tensor_sub(G16a[:], G16a[:], m2r2b[:]))
            v_x2 = vop(lambda E, c=c, hoff=hoff: E.tensor_add(
                fsl(Lb, c, hoff, 1024), G16a[:], fsl(Xb, c, hoff, 1024)))
    if debug:
        wait("se", "ve", v_x2)
        dma("se", "so", dbg["d_L"], Lb[:])

    h_buf = Xb
    v_outdma = None
    last_out_pe = 0
    for tau in range(4):
        mark("LN3t%d" % tau)
        toff = 512 * tau
        if v_outdma:
            waitt("ve", v_outdma)
        pe_s3 = 0
        for cb in range(4):
            wait("ve", "pe", pe_s3)
            if cb == 0:
                wait("ve", "ve", v_x2)
            gsrc = bass.AP(Lb.tensor, 2048 * 4 * cb + toff,
                           [[32768, 128], [2048, 4], [1, 512]])
            gdst = bass.AP(GSQ.tensor, 0, [[2048, 128], [512, 4], [1, 512]])
            vsq = vop(lambda E, a=gdst, b=gsrc: E.tensor_mul(a, b, b))
            wait("pe", "ve", vsq)
            aux_guard()
            fns = [lambda E: E.ldweights(onesb[:])]
            for j in range(4):
                c = 4 * cb + j
                fns.append(lambda E, c=c, toff=toff: E.matmul(
                    pR[0:1, 0:512], onesb[:], fsl(Lb, c, toff, 512),
                    start=(c == 0), stop=(c == 15)))
                fns.append(lambda E, j=j, c=c: E.matmul(
                    pT[0:1, 0:512], onesb[:], GSQ[:, 512 * j:512 * (j + 1)],
                    start=(c == 0), stop=(c == 15)))
            pe_s3 = peg(fns)
        wait("ve", "pe", pe_s3)
        vop(lambda E: E.tensor_copy(s34row[:], pR[0:1, 0:512]))
        vop(lambda E: E.tensor_copy(q34row[:], pT[0:1, 0:512]))
        vop(lambda E: E.tensor_scalar_mul(s34row[:], s34row[:], 1.0 / D))
        vop(lambda E: E.tensor_scalar_mul(q34row[:], q34row[:], 1.0 / D))
        vop(lambda E: E.tensor_mul(pT[0:1, 0:512], s34row[:], s34row[:]))
        vop(lambda E: E.tensor_sub(q34row[:], q34row[:], pT[0:1, 0:512]))
        v_v3 = vop(lambda E: E.tensor_scalar_add(q34row[:], q34row[:], EPS))
        wait("se", "ve", v_v3)
        s_r3 = sop(lambda E: E.activation(q34row[:], q34row[:], AF.Sqrt))
        wait("ve", "se", s_r3)
        v_r3 = vop(lambda E: E.reciprocal(q34row[:], q34row[:]))
        v_mr3 = vop(lambda E: E.tensor_mul(s34row[:], s34row[:], q34row[:]))
        v_b1 = bcast(q34row, r3b, 512, ("ve", v_r3))
        v_b2 = bcast(s34row, m3r3b, 512, ("ve", v_mr3))
        wait("ve", "ve", max(v_b1, v_b2))
        v_ln3 = 0
        for c in range(16):
            vop(lambda E, c=c, toff=toff: E.tensor_mul(
                Gf5a[:], fsl(Lb, c, toff, 512), r3b[:]))
            v_ln3 = vop(lambda E, c=c: E.tensor_sub(
                ln3f[:, 512 * c:512 * (c + 1)], Gf5a[:], m3r3b[:]))
        mark("C1t%d" % tau)
        pe_snapshot = cnt["pe"]
        pe_g = [0] * 64
        pe_u = [0] * 64
        h_ev = [0] * 64
        h_sil = [0] * 64
        for b in range(32):
            slot = b % 2
            wait("gp", "pe", pe_u[2 * (b - 2) + 1] if b >= 2 else pe_snapshot)
            if b == 0 and v_outdma:
                waitt("gp", v_outdma)
            gv = dma("gp", "wb%d" % slot, GB[:, slot], wg_h[:, 2 * b:2 * b + 2])
            uv = dma("gp", "wu%d" % slot, UB[:, slot], wu_h[:, 2 * b:2 * b + 2])
            waitt("pe", gv)
            waitt("pe", uv)
            for j in range(2):
                ic = 2 * b + j
                psl = ic % 2
                if ic >= 2:
                    wait("pe", "se", h_sil[ic - 2])
                    wait("pe", "ve", h_ev[ic - 2])
                if ic == 0:
                    wait("pe", "ve", v_ln3)
                gfns = []
                for hc in range(16):
                    gfns.append(lambda E, hc=hc, sl=slot, j=j: E.ldweights(
                        GB[:, sl, j, hc, :]))
                    gfns.append(lambda E, hc=hc, sl=slot, j=j, p=psl: E.matmul(
                        pM0[:, 512 * p:512 * (p + 1)], GB[:, sl, j, hc, :],
                        ln3f[:, 512 * hc:512 * (hc + 1)],
                        start=(hc == 0), stop=(hc == 15)))
                pe_g[ic] = peg(gfns)
                ufns = []
                for hc in range(16):
                    ufns.append(lambda E, hc=hc, sl=slot, j=j: E.ldweights(
                        UB[:, sl, j, hc, :]))
                    ufns.append(lambda E, hc=hc, sl=slot, j=j, p=psl: E.matmul(
                        pM1[:, 512 * p:512 * (p + 1)], UB[:, sl, j, hc, :],
                        ln3f[:, 512 * hc:512 * (hc + 1)],
                        start=(hc == 0), stop=(hc == 15)))
                pe_u[ic] = peg(ufns)
                wait("se", "pe", pe_g[ic])
                h_sil[ic] = sop(lambda E, ic=ic, p=psl: E.activation(
                    Gf5a[:] if p == 0 else Gf5b[:], pM0[:, 512 * p:512 * (p + 1)],
                    AF.Silu, scale=gsc[:, ic:ic + 1]))
                wait("ve", "pe", pe_u[ic])
                wait("ve", "se", h_sil[ic])
                h_ev[ic] = vop(lambda E, ic=ic, p=psl: E.scalar_tensor_tensor(
                    h_buf[:, 512 * ic:512 * (ic + 1)], pM1[:, 512 * p:512 * (p + 1)],
                    usc[:, ic:ic + 1], Gf5a[:] if p == 0 else Gf5b[:],
                    ALU.mult, ALU.mult))
        v_hdone = h_ev[63]
        wait("pe", "ve", v_hdone)
        aux_guard()
        sh_fns = []
        for ic in range(64):
            sh_fns.append(lambda E, ic=ic: E.ldweights(onesb[:]))
            sh_fns.append(lambda E, ic=ic: E.matmul(
                pR[0:1, 0:512], onesb[:], h_buf[:, 512 * ic:512 * (ic + 1)],
                start=(ic == 0), stop=(ic == 63)))
        pe_sh = peg(sh_fns)
        wait("ve", "pe", pe_sh)
        vop(lambda E: E.tensor_copy(shrow[:], pR[0:1, 0:512]))
        v_shneg = vop(lambda E: E.tensor_scalar_mul(shrow[:], shrow[:], -128.0))
        if debug and tau == 0:
            wait("se", "ve", v_hdone)
            dma("se", "so", dbg["d_ln3"], ln3f[:])
            dma("se", "so", dbg["d_h"], h_buf[:])
        mark("C2t%d" % tau)
        pe_s4 = 0
        dn_ev = [0] * 16
        dn_sil = [0] * 16
        dn_hist = [cnt["pe"], cnt["pe"]]
        for qq in range(4):
            pe_blk = [0] * 4
            for b in range(4):
                slot = b % 2
                wait("gp", "pe", dn_hist[slot])
                wv = dma("gp", "wd%d" % slot, DNB[:, slot],
                         wd_h[:, 16 * b:16 * b + 16, qq // 2,
                              4 * (qq % 2):4 * (qq % 2) + 4, :])
                waitt("pe", wv)
                if b == 0:
                    if qq == 0:
                        wait("pe", "ve", v_hdone)
                        wait("pe", "se", h_sil[63])
                    else:
                        wait("pe", "se", dn_sil[4 * qq - 1])
                dfns = []
                for f in range(4):
                    dst = pM0 if f < 2 else pM1
                    for il in range(16):
                        ic = 16 * b + il
                        dfns.append(lambda E, il=il, f=f, sl=slot: E.ldweights(
                            DNB[:, sl, il, f, :]))
                        dfns.append(lambda E, il=il, f=f, sl=slot, d=dst, ic=ic: E.matmul(
                            d[:, 512 * (f % 2):512 * (f % 2 + 1)], DNB[:, sl, il, f, :],
                            h_buf[:, 512 * ic:512 * (ic + 1)],
                            start=(ic == 0), stop=False))
                pe_blk[b] = peg(dfns)
                dn_hist[slot] = pe_blk[b]
            wait("pe", "ve", v_shneg)
            cfns = []
            for f in range(4):
                dst = pM0 if f < 2 else pM1
                cfns.append(lambda E, f=f, d=dst: E.matmul(
                    d[:, 512 * (f % 2):512 * (f % 2 + 1)], onesf[:], shrow[:],
                    start=False, stop=True))
            pe_corr = peg(cfns)
            dn_hist[0] = pe_corr
            dn_hist[1] = pe_corr
            for f in range(4):
                hf = 4 * qq + f
                src = pM0 if f < 2 else pM1
                wait("se", "pe", pe_corr)
                dn_sil[hf] = sop(lambda E, hf=hf, src=src, f=f: E.activation(
                    ffnf[:, 512 * hf:512 * (hf + 1)],
                    src[:, 512 * (f % 2):512 * (f % 2 + 1)],
                    AF.Identity, scale=dsc[:, hf:hf + 1]))
        pe_s4 = 0
        for cb in range(4):
            wait("ve", "pe", pe_s4)
            if cb == 0:
                wait("ve", "se", dn_sil[15])
            gsrc = bass.AP(ffnf.tensor, 2048 * cb, [[8192, 128], [512, 4], [1, 512]])
            gdst = bass.AP(GSQ.tensor, 0, [[2048, 128], [512, 4], [1, 512]])
            vsq = vop(lambda E, a=gdst, b=gsrc: E.tensor_mul(a, b, b))
            wait("pe", "ve", vsq)
            aux_guard()
            fns = [lambda E: E.ldweights(onesb[:])]
            for j in range(4):
                hf = 4 * cb + j
                fns.append(lambda E, hf=hf: E.matmul(
                    pR[0:1, 0:512], onesb[:], ffnf[:, 512 * hf:512 * (hf + 1)],
                    start=(hf == 0), stop=(hf == 15)))
                fns.append(lambda E, j=j, hf=hf: E.matmul(
                    pT[0:1, 0:512], onesb[:], GSQ[:, 512 * j:512 * (j + 1)],
                    start=(hf == 0), stop=(hf == 15)))
            pe_s4 = peg(fns)
        if debug and tau == 0:
            wait("se", "ve", dn_ev[15])
            dma("se", "so", dbg["d_ffn"], ffnf[:])
        mark("OUTt%d" % tau)
        wait("ve", "pe", pe_s4)
        vop(lambda E: E.tensor_copy(s34row[:], pR[0:1, 0:512]))
        vop(lambda E: E.tensor_copy(q34row[:], pT[0:1, 0:512]))
        vop(lambda E: E.tensor_scalar_mul(s34row[:], s34row[:], 1.0 / D))
        vop(lambda E: E.tensor_scalar_mul(q34row[:], q34row[:], 1.0 / D))
        vop(lambda E: E.tensor_mul(pT[0:1, 0:512], s34row[:], s34row[:]))
        vop(lambda E: E.tensor_sub(q34row[:], q34row[:], pT[0:1, 0:512]))
        v_v4 = vop(lambda E: E.tensor_scalar_add(q34row[:], q34row[:], EPS))
        wait("se", "ve", v_v4)
        s_r4 = sop(lambda E: E.activation(q34row[:], q34row[:], AF.Sqrt))
        wait("ve", "se", s_r4)
        v_r4 = vop(lambda E: E.reciprocal(q34row[:], q34row[:]))
        v_mr4 = vop(lambda E: E.tensor_mul(s34row[:], s34row[:], q34row[:]))
        v_b1 = bcast(q34row, r4b, 512, ("ve", v_r4))
        v_b2 = bcast(s34row, m4r4b, 512, ("ve", v_mr4))
        wait("ve", "ve", max(v_b1, v_b2))
        for fl in range(2):
            if v_outdma:
                waitt("ve", v_outdma)
            ev_slot = [0, 0, 0, 0]
            for c in range(16):
                co = 256 * fl
                vop(lambda E, c=c, co=co: E.tensor_mul(
                    G16a[:, 0:256], ffnf[:, 512 * c + co:512 * c + co + 256],
                    r4b[:, co:co + 256]))
                vop(lambda E, co=co: E.tensor_sub(
                    G16a[:, 0:256], G16a[:, 0:256], m4r4b[:, co:co + 256]))
                v_of = vop(lambda E, c=c, co=co, toff=toff: E.tensor_add(
                    Gf5b[:, 0:256], G16a[:, 0:256], fsl(Lb, c, toff + co, 256)))
                sl = c % 4
                wait("pe", "ve", v_of)
                if sl == 0 and c < 4:
                    aux_guard()
                else:
                    wait("pe", "ve", ev_slot[sl])
                fns = []
                for k in range(2):
                    fns.append(lambda E, sl=sl, k=k: E.transpose(
                        pT[:, 256 * sl + 128 * k:256 * sl + 128 * (k + 1)],
                        Gf5b[:, 128 * k:128 * (k + 1)], ident))
                ptv = peg(fns)
                last_out_pe = ptv
                wait("ve", "pe", ptv)
                dst = bass.AP(outTok.tensor, 128 * c, [[4096, 128], [2048, 2], [1, 128]])
                src = bass.AP(pT.tensor, 256 * sl, [[1024, 128], [128, 2], [1, 128]])
                ev_slot[sl] = vop(lambda E, d=dst, s=src: E.tensor_copy(d, s))
            wait("se", "ve", max(ev_slot))
            ydst = bass.AP(y_h.tensor, (toff + 256 * fl) * 2048,
                           [[2048, 128], [2048 * 128, 2], [1, 2048]])
            v_outdma = dma("se", "so", ydst, outTok[:])
    wait("se", "so", cnt["so"])

    def run_stream(E, eng):
        lst = q[eng]
        bounds = [(nm, idx[eng]) for nm, idx in marks] + [("_end", len(lst))]
        pos = 0
        seg = 0
        cur = "P0"
        while pos < len(lst):
            while seg < len(bounds) and bounds[seg][1] <= pos:
                cur = bounds[seg][0]
                seg += 1
            nxt = bounds[seg][1] if seg < len(bounds) else len(lst)
            if nxt == pos:
                seg += 1
                continue
            with nc.named_scope(cur):
                for f in lst[pos:nxt]:
                    f(E)
            pos = nxt

    with nc.Block() as block:
        @block.sync
        def _(E):
            run_stream(E, "sy")

        @block.gpsimd
        def _(E):
            run_stream(E, "gp")

        @block.tensor
        def _(E):
            run_stream(E, "pe")

        @block.vector
        def _(E):
            run_stream(E, "ve")

        @block.scalar
        def _(E):
            run_stream(E, "se")

    es.close()
    return nc


def _bfarr(a):
    return np.ascontiguousarray(a.astype(ml_dtypes.bfloat16))


def _tile_w(wt):
    K, M = wt.shape
    return np.ascontiguousarray(
        wt.reshape(K // 128, 128, M // 128, 128).transpose(1, 2, 0, 3))


def _cols(vec):
    return np.ascontiguousarray(np.asarray(vec).reshape(-1, 128).T)


def make_inputs(L0_attn_X, L0_attn_W_in, L0_attn_b_in, L0_attn_W_out, L0_attn_b_out,
                L0_ffn_gate_w, L0_ffn_gate_scale, L0_ffn_up_w, L0_ffn_up_scale,
                L0_ffn_down_w, L0_ffn_down_scale):
    X = np.ascontiguousarray(np.asarray(L0_attn_X, dtype=np.float32)).reshape(16384, 2048)
    W_in = np.asarray(L0_attn_W_in, dtype=np.float32)
    b_in = np.asarray(L0_attn_b_in, dtype=np.float32)
    W_out = np.asarray(L0_attn_W_out, dtype=np.float32)
    b_out = np.asarray(L0_attn_b_out, dtype=np.float32)
    gw = np.asarray(L0_ffn_gate_w, dtype=np.int32)
    uw = np.asarray(L0_ffn_up_w, dtype=np.int32)
    dw = np.asarray(L0_ffn_down_w, dtype=np.int32)
    gsv = np.asarray(L0_ffn_gate_scale, dtype=np.float32)
    usv = np.asarray(L0_ffn_up_scale, dtype=np.float32)
    dsv = np.asarray(L0_ffn_down_scale, dtype=np.float32)

    rot = 1024
    perm = np.concatenate([np.arange(0, rot, 2), np.arange(1, rot, 2),
                           np.arange(rot, 2048)])
    colmap = np.concatenate([perm, 2048 + perm, 4096 + perm])

    wqkv = _tile_w(np.ascontiguousarray(W_in.T[:, colmap]))
    bqkv = _cols(b_in[colmap])
    woutT = np.ascontiguousarray(W_out.T[perm, :])
    wout = _tile_w(woutT)
    bov = _cols(b_out)
    wg = _tile_w(np.ascontiguousarray(gw.T))
    wu = _tile_w(np.ascontiguousarray(uw.T))
    wdT = np.ascontiguousarray(dw.T)
    wd = np.ascontiguousarray(
        wdT.reshape(64, 128, 2, 8, 128).transpose(1, 0, 2, 3, 4))
    gsc = _cols(gsv)
    usc = _cols(usv)
    dsc = _cols(dsv)
    ident = np.eye(128, dtype=np.float32)

    inv_freq = (1.0 / (10000.0 ** (np.arange(0, rot, 2, dtype=np.float64) / rot)))

    in_maps = []
    for core in range(8):
        p0 = (core % 2) * 2048
        pos = np.arange(p0, p0 + 2048, dtype=np.float64)
        ang = pos[None, :] * inv_freq[:, None]
        cosc = _bfarr(np.cos(ang).reshape(4, 128, 2048).transpose(1, 0, 2))
        sinc = _bfarr(np.sin(ang).reshape(4, 128, 2048).transpose(1, 0, 2))
        in_maps.append({
            "x": np.ascontiguousarray(X[2048 * core:2048 * (core + 1)]),
            "wqkv": wqkv, "wout": wout, "wg": wg, "wu": wu, "wd": wd,
            "bqkv": bqkv, "bo": bov, "gs": gsc, "us": usc, "ds": dsc,
            "cost": cosc, "sint": sinc, "idn": ident,
        })
    return in_maps


def kernel(**inputs):
    in_maps = make_inputs(**inputs)
    nc = build_program(debug=False)
    res = run_bass_kernel_spmd(nc, in_maps, core_ids=list(range(8)))
    out = np.concatenate([np.asarray(r["y"]) for r in res.results], axis=0)
    return out.reshape(4, 4096, 2048)


# version 10
# speedup vs baseline: 1.0504x; 1.0504x over previous
import numpy as np
import ml_dtypes
import concourse.bass as bass
import concourse.mybir as mybir
from concourse.bass_utils import run_bass_kernel_spmd

F32 = mybir.dt.float32
BF16 = mybir.dt.bfloat16
I32 = mybir.dt.int32
AF = mybir.ActivationFunctionType
ALU = mybir.AluOpType
AX = mybir.AxisListType

D = 2048
T = 2048
EPS = 1e-5
KB = 1024
SB0 = 16512

OFF_X = SB0
OFF_L = SB0 + 64 * KB
OFF_KQ = SB0 + 128 * KB
OFF_TAB = SB0 + 160 * KB
OFF_WB = SB0 + 176 * KB
OFF_WD = SB0 + 184 * KB
OFF_VW = SB0 + 188 * KB
OFF_F = SB0 + 192 * KB
OFF_G = SB0 + 200 * KB
OFF_TC = SB0 + 204 * KB


def build_program(debug=False):
    nc = bass.Bass("TRN2", target_bir_lowering=False)

    x_h = nc.dram_tensor("x", [T, D], F32, kind="ExternalInput").ap()
    wqkv_h = nc.dram_tensor("wqkv", [128, 48, 16, 128], F32, kind="ExternalInput").ap()
    wout_h = nc.dram_tensor("wout", [128, 16, 16, 128], F32, kind="ExternalInput").ap()
    wg_h = nc.dram_tensor("wg", [128, 64, 16, 128], I32, kind="ExternalInput").ap()
    wu_h = nc.dram_tensor("wu", [128, 64, 16, 128], I32, kind="ExternalInput").ap()
    wd_h = nc.dram_tensor("wd", [128, 64, 2, 8, 128], I32, kind="ExternalInput").ap()
    bqkv_h = nc.dram_tensor("bqkv", [128, 48], F32, kind="ExternalInput").ap()
    bo_h = nc.dram_tensor("bo", [128, 16], F32, kind="ExternalInput").ap()
    gs_h = nc.dram_tensor("gs", [128, 64], F32, kind="ExternalInput").ap()
    us_h = nc.dram_tensor("us", [128, 64], F32, kind="ExternalInput").ap()
    ds_h = nc.dram_tensor("ds", [128, 16], F32, kind="ExternalInput").ap()
    cos_h = nc.dram_tensor("cost", [128, 4, 2048], BF16, kind="ExternalInput").ap()
    sin_h = nc.dram_tensor("sint", [128, 4, 2048], BF16, kind="ExternalInput").ap()
    idn_h = nc.dram_tensor("idn", [128, 128], F32, kind="ExternalInput").ap()
    y_h = nc.dram_tensor("y", [T, D], F32, kind="ExternalOutput").ap()

    a_in = nc.dram_tensor("a_in", [128, 16], F32)
    a_out = nc.dram_tensor("a_out", [128, 16], F32)

    dbg = {}
    if debug:
        for nm, shp, dt in [("d_xf", [128, 32768], BF16), ("d_ln1", [128, 32768], BF16),
                            ("d_w", [1, 1024], F32), ("d_apar", [128, 16], F32),
                            ("d_afull", [128, 16], F32), ("d_kq", [128, 16384], BF16),
                            ("d_L", [128, 32768], BF16), ("d_h", [128, 32768], BF16),
                            ("d_ln3", [128, 8192], BF16), ("d_ffn", [128, 8192], BF16),
                            ("d_wbc", [128, 1024], F32), ("d_vw", [128, 1024], BF16),
                            ("d_kp", [128, 1024], BF16), ("d_kv0", [128, 1024], BF16),
                            ("d_ap0", [128, 16], F32), ("d_cos", [128, 4, 1024], BF16)]:
            dbg[nm] = nc.dram_tensor(nm, shp, dt, kind="ExternalOutput").ap()

    AL = nc.alloc_sbuf_tensor_at
    Xb = AL("Xb", [128, 32768], BF16, offset=OFF_X).ap()
    Lb = AL("Lb", [128, 32768], BF16, offset=OFF_L).ap()
    XT = AL("XT", [128, 2, 2048], F32, offset=OFF_L).ap()
    SQd = AL("SQd", [128, 2048], F32, offset=OFF_L + 16 * KB).ap()
    KQ = AL("KQ", [128, 16384], BF16, offset=OFF_KQ).ap()
    r1row = AL("r1row", [1, 2048], F32, offset=OFF_KQ).ap()
    mr1row = AL("mr1row", [1, 2048], F32, offset=OFF_KQ + 8 * KB).ap()
    st1 = AL("st1", [16, 128], F32, offset=OFF_KQ + 16 * KB).ap()
    st2 = AL("st2", [16, 128], F32, offset=OFF_KQ + 16 * KB + 512).ap()
    BC1 = AL("BC1", [128, 2048], BF16, offset=OFF_KQ + 16 * KB).ap()
    BC2 = AL("BC2", [128, 2048], BF16, offset=OFF_KQ + 24 * KB).ap()
    ln3f = AL("ln3f", [128, 8192], BF16, offset=OFF_KQ).ap()
    ffnf = AL("ffnf", [128, 8192], BF16, offset=OFF_KQ + 16 * KB).ap()
    outTok = AL("outTok", [128, 2, 2048], F32, offset=OFF_X + 48 * KB).ap()
    outTok4 = AL("outTok4", [128, 4, 2048], F32, offset=OFF_TAB).ap()
    COSH = AL("COSH", [128, 4, 1024], BF16, offset=OFF_TAB).ap()
    SINH = AL("SINH", [128, 4, 1024], BF16, offset=OFF_TAB + 8 * KB).ap()
    WU2 = AL("WU2", [128, 2, 16, 128], BF16, offset=OFF_TAB).ap()
    shrow = AL("shrow", [1, 512], F32, offset=OFF_F + 1 * KB).ap()
    GB = AL("GB", [128, 2, 2, 16, 128], BF16, offset=OFF_TAB).ap()
    UB = AL("UB", [128, 2, 2, 16, 128], BF16, offset=OFF_WB).ap()
    DNB = AL("DNB", [128, 2, 16, 4, 128], BF16, offset=OFF_TAB).ap()
    WB = AL("WB", [128, 2, 16, 128], BF16, offset=OFF_WB).ap()
    s2row = AL("s2row", [1, 1024], F32, offset=OFF_VW).ap()
    q2row = AL("q2row", [1, 1024], F32, offset=OFF_G).ap()
    WD2 = AL("WD2", [128, 2, 4, 128], BF16, offset=OFF_WD).ap()
    s34row = AL("s34row", [1, 512], F32, offset=OFF_VW).ap()
    q34row = AL("q34row", [1, 512], F32, offset=OFF_VW + 2 * KB).ap()
    VW = AL("VW", [128, 2, 1024], BF16, offset=OFF_VW).ap()
    wrow = AL("wrow", [1, 1024], F32, offset=OFF_VW).ap()
    WBC = AL("WBC", [128, 1024], F32, offset=OFF_F).ap()
    r2b = AL("r2b", [128, 1024], BF16, offset=OFF_F).ap()
    m2r2b = AL("m2r2b", [128, 1024], BF16, offset=OFF_F + 4 * KB).ap()
    r3b = AL("r3b", [128, 512], BF16, offset=OFF_F).ap()
    m3r3b = AL("m3r3b", [128, 512], BF16, offset=OFF_F + 2 * KB).ap()
    r4b = AL("r4b", [128, 512], BF16, offset=OFF_F + 4 * KB).ap()
    m4r4b = AL("m4r4b", [128, 512], BF16, offset=OFF_F + 6 * KB).ap()
    GSQ = AL("GSQ", [128, 2048], BF16, offset=OFF_G).ap()
    G16a = AL("G16a", [128, 1024], BF16, offset=OFF_G).ap()
    G16b = AL("G16b", [128, 1024], BF16, offset=OFF_G + 2 * KB).ap()
    Gf = AL("Gf", [128, 1024], F32, offset=OFF_G).ap()
    Gf5a = AL("Gf5a", [128, 512], F32, offset=OFF_G).ap()
    Gf5b = AL("Gf5b", [128, 512], F32, offset=OFF_G + 2 * KB).ap()

    t0 = OFF_TC
    s1c = AL("s1c", [128, 16], F32, offset=t0).ap()
    sqc = AL("sqc", [128, 16], F32, offset=t0 + 64).ap()
    m1c = AL("m1c", [128, 16], F32, offset=t0 + 128).ap()
    r1c = AL("r1c", [128, 16], F32, offset=t0 + 192).ap()
    mr1c = AL("mr1c", [128, 16], F32, offset=t0 + 256).ap()
    bqkv = AL("bqkvs", [128, 48], F32, offset=t0 + 320).ap()
    bo = AL("bos", [128, 16], F32, offset=t0 + 512).ap()
    gsc = AL("gscs", [128, 64], F32, offset=t0 + 576).ap()
    usc = AL("uscs", [128, 64], F32, offset=t0 + 832).ap()
    dsc = AL("dscs", [128, 16], F32, offset=t0 + 1088).ap()
    Apar = AL("Apar", [128, 16], F32, offset=t0 + 1152).ap()
    Afull = AL("Afull", [128, 16], F32, offset=t0 + 1216).ap()
    bA = AL("bAs", [128, 16], F32, offset=t0 + 1280).ap()
    rcol = AL("rcol", [128, 1], F32, offset=t0 + 1344).ap()
    onesb = AL("onesb", [128, 1], BF16, offset=t0 + 1344 + 32).ap()
    epsc = AL("epsc", [128, 1], F32, offset=t0 + 2432).ap()
    eps24 = AL("eps24", [128, 1], F32, offset=t0 + 2464).ap()
    onesf = AL("onesf", [1, 128], F32, offset=t0 + 1408).ap()
    ident = AL("idents", [128, 128], F32, offset=t0 + 1920).ap()

    pM0 = nc.alloc_psum_tensor("pM0", [128, 1024], F32).ap()
    pM1 = nc.alloc_psum_tensor("pM1", [128, 1024], F32).ap()
    pT = nc.alloc_psum_tensor("pT", [128, 1024], F32).ap()
    pR = nc.alloc_psum_tensor("pR", [128, 1024], F32).ap()
    pM = [pM0, pM1]

    import contextlib
    from collections import defaultdict
    es = contextlib.ExitStack()
    S = {}
    for nm in ("x0", "x1", "tb", "ct", "wb0", "wb1", "wu0", "wu1", "wd0", "wd1",
               "gr", "cc", "gc", "pe", "ve", "se", "so"):
        S[nm] = es.enter_context(nc.semaphore(nm))
    cnt = defaultdict(int)
    q = {"sy": [], "gp": [], "pe": [], "ve": [], "se": []}
    marks = []

    def mark(name):
        marks.append((name, {e: len(q[e]) for e in q}))

    def emit(eng, fn):
        q[eng].append(fn)

    def wait(eng, sem, val):
        if val > 0:
            emit(eng, lambda E, s=S[sem], v=val: E.wait_ge(s, v))

    def dma(engq, sem, out, in_):
        if cnt[sem] > 0:
            wait(engq, sem, cnt[sem])
        cnt[sem] += 16
        emit(engq, lambda E, o=out, i=in_, s=S[sem]:
             E.dma_start(out=o, in_=i).then_inc(s, 16))
        return (sem, cnt[sem])

    def waitt(eng, tok):
        if tok:
            wait(eng, tok[0], tok[1])

    def vop(fn):
        wait("ve", "ve", cnt["ve"])
        cnt["ve"] += 1
        emit("ve", lambda E, f=fn, s=S["ve"]: f(E).then_inc(s, 1))
        return cnt["ve"]

    def sop(fn):
        wait("se", "se", cnt["se"])
        cnt["se"] += 1
        emit("se", lambda E, f=fn, s=S["se"]: f(E).then_inc(s, 1))
        return cnt["se"]

    def gop(fn):
        wait("gp", "gc", cnt["gc"])
        cnt["gc"] += 1
        emit("gp", lambda E, f=fn, s=S["gc"]: f(E).then_inc(s, 1))
        return cnt["gc"]

    def peg(fns):
        for f in fns[:-1]:
            emit("pe", f)
        cnt["pe"] += 1
        emit("pe", lambda E, f=fns[-1], s=S["pe"]: f(E).then_inc(s, 1))
        return cnt["pe"]

    def aux_guard():
        wait("pe", "ve", cnt["ve"])
        wait("pe", "se", cnt["se"])

    def fsl(buf, c, off, n):
        return buf[:, 2048 * c + off: 2048 * c + off + n]

    def kqsl(c, off, n):
        return KQ[:, 1024 * c + off: 1024 * c + off + n]

    for out, in_ in [(bqkv, bqkv_h), (bo, bo_h), (gsc, gs_h), (usc, us_h),
                     (dsc, ds_h), (ident, idn_h)]:
        v_const = dma("sy", "ct", out, in_)
    waitt("pe", v_const)
    waitt("ve", v_const)
    waitt("se", v_const)
    vop(lambda E: E.memset(onesb[:], 1.0))
    vop(lambda E: E.memset(onesf[:], 1.0))
    vop(lambda E: E.memset(Apar[:], 0.0))
    vop(lambda E: E.memset(epsc[:], EPS))
    vop(lambda E: E.memset(eps24[:], 1e-24))

    mark("P1")
    ev_batch = 0
    cons = [(0, 0, 0)] * 16
    for tc in range(16):
        if tc >= 2:
            p, s, v = cons[tc - 2]
            wait("sy", "pe", p)
            wait("sy", "se", s)
            wait("sy", "ve", v)
        xv = dma("sy", "x%d" % (tc % 2), XT[:, tc % 2, :],
                 x_h[128 * tc:128 * (tc + 1), :])
        waitt("se", xv)
        se_sq = sop(lambda E, tc=tc: E.activation(
            SQd[:], XT[:, tc % 2, :], AF.Square, accum_out=sqc[:, tc:tc + 1]))
        waitt("ve", xv)
        ve_rs = vop(lambda E, tc=tc: E.reduce_sum(
            s1c[:, tc:tc + 1], XT[:, tc % 2, :], axis=AX.X))
        pe_last = 0
        for b in range(2):
            waitt("pe", xv)
            aux_guard()
            fns = []
            for j in range(8):
                c = 8 * b + j
                fns.append(lambda E, tc=tc, c=c, j=j: E.transpose(
                    pT[:, 128 * j:128 * (j + 1)],
                    XT[:, tc % 2, 128 * c:128 * (c + 1)], ident))
            pe_last = peg(fns)
            wait("ve", "pe", pe_last)
            dst = bass.AP(Xb.tensor, 2048 * 8 * b + 128 * tc,
                          [[32768, 128], [2048, 8], [1, 128]])
            src = bass.AP(pT.tensor, 0, [[1024, 128], [128, 8], [1, 128]])
            ev_batch = vop(lambda E, d=dst, s=src: E.tensor_copy(d, s))
        cons[tc] = (pe_last, se_sq, max(ve_rs, ev_batch))

    wait("ve", "se", cnt["se"])
    vop(lambda E: E.tensor_scalar_mul(m1c[:], s1c[:], 1.0 / D))
    vop(lambda E: E.tensor_scalar_mul(r1c[:], sqc[:], 1.0 / D))
    vop(lambda E: E.tensor_mul(mr1c[:], m1c[:], m1c[:]))
    vop(lambda E: E.tensor_sub(r1c[:], r1c[:], mr1c[:]))
    v_var = vop(lambda E: E.tensor_scalar_add(r1c[:], r1c[:], EPS))
    wait("se", "ve", v_var)
    s_sd = sop(lambda E: E.activation(r1c[:], r1c[:], AF.Sqrt))
    wait("ve", "se", s_sd)
    vop(lambda E: E.reciprocal(r1c[:], r1c[:]))
    v_r1 = vop(lambda E: E.tensor_mul(mr1c[:], m1c[:], r1c[:]))
    wait("pe", "ve", v_r1)
    aux_guard()
    p_st = peg([lambda E: E.transpose(pT[0:16, 0:128], r1c[:], ident),
                lambda E: E.transpose(pT[0:16, 128:256], mr1c[:], ident)])
    wait("ve", "pe", p_st)
    vop(lambda E: E.tensor_copy(st1[:], pT[0:16, 0:128]))
    v_st = vop(lambda E: E.tensor_copy(st2[:], pT[0:16, 128:256]))
    wait("gp", "ve", v_st)
    v_row1 = dma("gp", "gr",
                 bass.AP(r1row.tensor, 0, [[2048, 1], [128, 16], [1, 128]]), st1[:])
    v_row2 = dma("gp", "gr",
                 bass.AP(mr1row.tensor, 0, [[2048, 1], [128, 16], [1, 128]]), st2[:])

    def bcast(row, dst, width, ready):
        vv = 0
        pos = 0
        while pos < width:
            w = min(1024, width - pos)
            wait("pe", ready[0], ready[1])
            aux_guard()
            fns = []
            for n in range(0, w, 512):
                nn = min(512, w - n)
                fns.append(lambda E, p=pos, n=n, nn=nn: E.matmul(
                    pT[:, n:n + nn], onesf[:],
                    bass.AP(row.tensor, row.offset + p + n, [[width, 1], [1, nn]]),
                    start=True, stop=True))
            pv = peg(fns)
            wait("ve", "pe", pv)
            vv = vop(lambda E, p=pos, w=w, d=dst: E.tensor_copy(d[:, p:p + w], pT[:, 0:w]))
            pos += w
        return vv

    waitt("ve", v_row2)
    v_bc1 = bcast(r1row, BC1, 2048, v_row1)
    v_bc2 = bcast(mr1row, BC2, 2048, v_row2)

    mark("P2")
    v_ln1 = 0
    for c in range(16):
        vop(lambda E, c=c: E.tensor_mul(fsl(Lb, c, 0, 2048), fsl(Xb, c, 0, 2048), BC1[:]))
        v_ln1 = vop(lambda E, c=c: E.tensor_sub(fsl(Lb, c, 0, 2048),
                                                fsl(Lb, c, 0, 2048), BC2[:]))
    if debug:
        wait("se", "ve", v_ln1)
        dma("se", "so", dbg["d_xf"], Xb[:])
        dma("se", "so", dbg["d_ln1"], Lb[:])

    mm = {"evac": [0, 0], "par": 0}

    def mm_group(wb_t, wb_slot, wtok, moving_fn, extra=None):
        par = mm["par"]
        mm["par"] ^= 1
        buf = pM[par]
        wait("pe", wtok[0], wtok[1])
        if mm["evac"][par]:
            wait("pe", mm["evac"][par][0], mm["evac"][par][1])
        if extra is not None:
            wait("pe", extra[0], extra[1])
        fns = []
        for dc in range(16):
            fns.append(lambda E, dc=dc, t=wb_t, s=wb_slot: E.ldweights(t[:, s, dc, :]))
            for n in (0, 512):
                fns.append(lambda E, dc=dc, n=n, b=buf, t=wb_t, s=wb_slot: E.matmul(
                    b[:, n:n + 512], t[:, s, dc, :], moving_fn(dc, n),
                    start=(dc == 0), stop=(dc == 15)))
        return par, buf, peg(fns)

    def set_evac(par, v):
        mm["evac"][par] = v

    v_apar = 0
    last_pe = 0
    v_tab_use = 0
    for h in range(2):
        mark("P3h%d" % h)
        hoff = 1024 * h
        wait("ve", "ve", max(v_bc1, v_bc2, v_ln1))
        wait("sy", "ve", v_tab_use)
        dma("sy", "tb", COSH[:], cos_h[:, :, hoff:hoff + 1024])
        v_tab = dma("sy", "tb", SINH[:], sin_h[:, :, hoff:hoff + 1024])
        pe_grp = [0] * 16
        pe_sig = 0
        for c in range(16):
            g = 16 + c
            slot = c % 2
            after = pe_grp[c - 2] if c >= 2 else last_pe
            wait("gp", "pe", after)
            wv = dma("gp", "wb%d" % slot, WB[:, slot, :, :], wqkv_h[:, g])
            par, buf, pv = mm_group(WB, slot, wv,
                                    lambda dc, n, hoff=hoff: fsl(Lb, dc, hoff + n, 512),
                                    extra=("ve", v_ln1) if (h == 0 and c == 0) else None)
            pe_grp[c] = pv
            wait("se", "pe", pv)
            sv = sop(lambda E, c=c, g=g, b=buf: E.activation(
                kqsl(c, 0, 1024), b[:], AF.Identity, bias=bqkv[:, g:g + 1], scale=1.0))
            set_evac(par, ("se", sv))
            if c % 2 == 1:
                wait("ve", "pe", pe_sig)
                wait("ve", "se", sv)
                ksrc = bass.AP(KQ.tensor, 1024 * (c - 1),
                               [[16384, 128], [1024, 2], [1, 1024]])
                kdst = bass.AP(GSQ.tensor, 0, [[2048, 128], [1024, 2], [1, 1024]])
                vsq = vop(lambda E, a=kdst, b=ksrc: E.tensor_mul(a, b, b))
                wait("pe", "ve", vsq)
                aux_guard()
                fns = [lambda E: E.ldweights(onesb[:])]
                for cc in (c - 1, c):
                    o = 1024 * (cc - c + 1)
                    fns.append(lambda E, cc=cc, o=o: E.matmul(
                        pR[0:1, 0:512], onesb[:], GSQ[:, o:o + 512],
                        start=(cc == 0), stop=(cc == 15)))
                    fns.append(lambda E, cc=cc, o=o: E.matmul(
                        pR[0:1, 512:1024], onesb[:], GSQ[:, o + 512:o + 1024],
                        start=(cc == 0), stop=(cc == 15)))
                pe_sig = peg(fns)
        wait("se", "pe", pe_sig)
        ssd = sop(lambda E: E.activation(pR[0:1, 0:1024], pR[0:1, 0:1024], AF.Sqrt))
        wait("ve", "se", ssd)
        vop(lambda E: E.tensor_scalar_max(pR[0:1, 0:1024], pR[0:1, 0:1024], 1e-12))
        v_w = vop(lambda E: E.reciprocal(wrow[:], pR[0:1, 0:1024]))
        if debug and h == 0:
            wait("se", "ve", v_w)
            tdw = dma("se", "so", dbg["d_w"], wrow[:])
            waitt("ve", tdw)
        v_wb = bcast(wrow, WBC, 1024, ("ve", v_w))
        for c in range(16):
            g = 32 + c
            slot = c % 2
            after = pe_grp[c - 2] if c >= 2 else pe_grp[15]
            wait("gp", "pe", after)
            wv = dma("gp", "wb%d" % slot, WB[:, slot, :, :], wqkv_h[:, g])
            par, buf, pv = mm_group(WB, slot, wv,
                                    lambda dc, n, hoff=hoff: fsl(Lb, dc, hoff + n, 512))
            pe_grp[c] = pv
            vpar = c % 2
            wait("ve", "pe", pv)
            wait("ve", "ve", v_wb)
            vv = vop(lambda E, g=g, b=buf, vp=vpar: E.scalar_tensor_tensor(
                VW[:, vp, :], b[:], bqkv[:, g:g + 1], WBC[:], ALU.add, ALU.mult))
            set_evac(par, ("ve", vv))
            if debug and h == 0 and c == 0:
                wait("se", "ve", vv)
                t1 = dma("se", "so", dbg["d_wbc"], WBC[:])
                t2 = dma("se", "so", dbg["d_vw"], VW[:, 0, :])
                waitt("ve", t2)
            if c == 0:
                waitt("ve", v_tab)
            if c < 4:
                vop(lambda E, c=c: E.tensor_mul(
                    G16a[:], kqsl(c, 0, 1024), COSH[:, c, :]))
                vop(lambda E, c=c: E.tensor_mul(
                    G16b[:], kqsl(c + 4, 0, 1024), SINH[:, c, :]))
                vop(lambda E: E.tensor_sub(G16a[:], G16a[:], G16b[:]))
                vop(lambda E, vp=vpar: E.tensor_mul(G16b[:], G16a[:], VW[:, vp, :]))
            elif c < 8:
                vop(lambda E, c=c: E.tensor_mul(
                    G16a[:], kqsl(c, 0, 1024), COSH[:, c - 4, :]))
                vop(lambda E, c=c: E.tensor_mul(
                    G16b[:], kqsl(c - 4, 0, 1024), SINH[:, c - 4, :]))
                vop(lambda E: E.tensor_add(G16a[:], G16a[:], G16b[:]))
                vop(lambda E, vp=vpar: E.tensor_mul(G16b[:], G16a[:], VW[:, vp, :]))
            else:
                vop(lambda E, c=c, vp=vpar: E.tensor_mul(
                    G16b[:], kqsl(c, 0, 1024), VW[:, vp, :]))
            if debug and h == 0 and c == 0:
                wait("se", "ve", cnt["ve"])
                dma("se", "so", dbg["d_cos"], COSH[:])
                dma("se", "so", dbg["d_kp"], G16a[:])
                t3 = dma("se", "so", dbg["d_kv0"], G16b[:])
                waitt("ve", t3)
            vop(lambda E: E.reduce_sum(rcol[:], G16b[:], axis=AX.X))
            v_apar = vop(lambda E, c=c: E.tensor_add(
                Apar[:, c:c + 1], Apar[:, c:c + 1], rcol[:]))
            v_tab_use = v_apar
            if debug and h == 0 and c == 0:
                wait("se", "ve", v_apar)
                t4 = dma("se", "so", dbg["d_ap0"], Apar[:])
                waitt("ve", t4)
        last_pe = pe_grp[15]

    if debug:
        wait("se", "ve", v_apar)
        dma("se", "so", dbg["d_apar"], Apar[:])

    mark("P4")
    wait("gp", "ve", v_apar)
    v_gb = dma("gp", "gr", a_in.ap(), Apar[:])
    waitt("gp", v_gb)
    cnt["cc"] += 1
    emit("gp", lambda E: E.collective_compute(
        "AllReduce", ALU.add, replica_groups=[[0, 1], [2, 3], [4, 5], [6, 7]],
        ins=[a_in.ap().opt()], outs=[a_out.ap().opt()]).then_inc(S["cc"], 1))
    v_afull = None
    v_bA = None

    v_x2 = 0
    for h in range(2):
        mark("P5h%d" % h)
        hoff = 1024 * h
        wait("sy", "ve", v_tab_use)
        dma("sy", "tb", COSH[:], cos_h[:, :, hoff:hoff + 1024])
        v_tab = dma("sy", "tb", SINH[:], sin_h[:, :, hoff:hoff + 1024])
        pe_grp = [0] * 16
        v_qe = 0
        for c in range(16):
            slot = c % 2
            after = pe_grp[c - 2] if c >= 2 else last_pe
            wait("gp", "pe", after)
            wv = dma("gp", "wb%d" % slot, WB[:, slot, :, :], wqkv_h[:, c])
            par, buf, pv = mm_group(WB, slot, wv,
                                    lambda dc, n, hoff=hoff: fsl(Lb, dc, hoff + n, 512))
            pe_grp[c] = pv
            wait("se", "pe", pv)
            if c >= 8:
                wait("se", "ve", v_bA)
                sv = sop(lambda E, c=c, b=buf: E.activation(
                    kqsl(c, 0, 1024), b[:], AF.Identity, bias=bA[:, c:c + 1],
                    scale=Afull[:, c:c + 1]))
            else:
                sv = sop(lambda E, c=c, b=buf: E.activation(
                    kqsl(c, 0, 1024), b[:], AF.Identity, bias=bqkv[:, c:c + 1],
                    scale=1.0))
            set_evac(par, ("se", sv))
            v_qe = sv
            if h == 0 and c == 3 and v_afull is None:
                wait("gp", "cc", cnt["cc"])
                v_afull = dma("gp", "gr", Afull[:], a_out.ap())
                waitt("gp", v_afull)
                if debug:
                    waitt("se", v_afull)
                    dma("se", "so", dbg["d_afull"], Afull[:])
                waitt("ve", v_afull)
                v_bA = vop(lambda E: E.tensor_mul(bA[:], bqkv[:, 0:16], Afull[:]))
        last_pe = pe_grp[15]
        wait("ve", "se", v_qe)
        waitt("ve", v_tab)
        v_y = 0
        for j in range(4):
            vop(lambda E, j=j: E.scalar_tensor_tensor(
                G16a[:], kqsl(j, 0, 1024), Afull[:, j:j + 1],
                COSH[:, j, :], ALU.mult, ALU.mult))
            vop(lambda E, j=j: E.scalar_tensor_tensor(
                G16b[:], kqsl(j + 4, 0, 1024), Afull[:, j:j + 1],
                SINH[:, j, :], ALU.mult, ALU.mult))
            vop(lambda E, j=j: E.scalar_tensor_tensor(
                VW[:, 0, :], kqsl(j + 4, 0, 1024), Afull[:, j + 4:j + 5],
                COSH[:, j, :], ALU.mult, ALU.mult))
            vop(lambda E, j=j: E.scalar_tensor_tensor(
                VW[:, 1, :], kqsl(j, 0, 1024), Afull[:, j + 4:j + 5],
                SINH[:, j, :], ALU.mult, ALU.mult))
            vop(lambda E, j=j: E.tensor_sub(kqsl(j, 0, 1024), G16a[:], G16b[:]))
            v_y = vop(lambda E, j=j: E.tensor_add(kqsl(j + 4, 0, 1024),
                                                  VW[:, 0, :], VW[:, 1, :]))
        v_tab_use = v_y
        if debug and h == 0:
            wait("se", "ve", v_y)
            tkq = dma("se", "so", dbg["d_kq"], KQ[:])
            waitt("se", tkq)
        mark("P6h%d" % h)
        pe_s2 = 0
        v_at = 0
        for g in range(16):
            slot = g % 2
            after = pe_grp[g - 2] if g >= 2 else last_pe
            wait("gp", "pe", after)
            wv = dma("gp", "wb%d" % slot, WB[:, slot, :, :], wout_h[:, g])
            par, buf, pv = mm_group(WB, slot, wv,
                                    lambda dc, n: kqsl(dc, n, 512),
                                    extra=("ve", v_y) if g == 0 else None)
            pe_grp[g] = pv
            wait("ve", "pe", pv)
            v_at = vop(lambda E, g=g, b=buf, hoff=hoff: E.scalar_tensor_tensor(
                fsl(Lb, g, hoff, 1024), b[:], bo[:, g:g + 1],
                fsl(Lb, g, hoff, 1024), ALU.add, ALU.add))
            set_evac(par, ("ve", v_at))
            wait("ve", "pe", pe_s2)
            vsq = vop(lambda E, g=g, hoff=hoff: E.tensor_mul(
                G16a[:], fsl(Lb, g, hoff, 1024), fsl(Lb, g, hoff, 1024)))
            wait("pe", "ve", vsq)
            aux_guard()
            pe_s2 = peg([lambda E: E.ldweights(onesb[:]),
                         lambda E, g=g, hoff=hoff: E.matmul(
                             pR[0:1, 0:512], onesb[:], fsl(Lb, g, hoff, 512),
                             start=(g == 0), stop=(g == 15)),
                         lambda E, g=g, hoff=hoff: E.matmul(
                             pR[0:1, 512:1024], onesb[:], fsl(Lb, g, hoff + 512, 512),
                             start=(g == 0), stop=(g == 15)),
                         lambda E, g=g: E.matmul(
                             pT[0:1, 0:512], onesb[:], G16a[:, 0:512],
                             start=(g == 0), stop=(g == 15)),
                         lambda E, g=g: E.matmul(
                             pT[0:1, 512:1024], onesb[:], G16a[:, 512:1024],
                             start=(g == 0), stop=(g == 15))])
        last_pe = pe_grp[15]
        wait("ve", "pe", pe_s2)
        vop(lambda E: E.tensor_copy(s2row[:], pR[0:1, 0:1024]))
        vop(lambda E: E.tensor_copy(q2row[:], pT[0:1, 0:1024]))
        vop(lambda E: E.tensor_scalar_mul(s2row[:], s2row[:], 1.0 / D))
        vop(lambda E: E.tensor_scalar_mul(q2row[:], q2row[:], 1.0 / D))
        vop(lambda E: E.tensor_mul(pT[0:1, 0:1024], s2row[:], s2row[:]))
        vop(lambda E: E.tensor_sub(q2row[:], q2row[:], pT[0:1, 0:1024]))
        v_v2 = vop(lambda E: E.tensor_scalar_add(q2row[:], q2row[:], EPS))
        wait("se", "ve", v_v2)
        s_r2 = sop(lambda E: E.activation(q2row[:], q2row[:], AF.Sqrt))
        wait("ve", "se", s_r2)
        v_r2 = vop(lambda E: E.reciprocal(q2row[:], q2row[:]))
        v_mr2 = vop(lambda E: E.tensor_mul(s2row[:], s2row[:], q2row[:]))
        v_b1 = bcast(q2row, r2b, 1024, ("ve", v_r2))
        v_b2 = bcast(s2row, m2r2b, 1024, ("ve", v_mr2))
        wait("ve", "ve", max(v_b1, v_b2))
        for c in range(16):
            vop(lambda E, c=c, hoff=hoff: E.tensor_mul(
                G16a[:], fsl(Lb, c, hoff, 1024), r2b[:]))
            vop(lambda E: E.tensor_sub(G16a[:], G16a[:], m2r2b[:]))
            v_x2 = vop(lambda E, c=c, hoff=hoff: E.tensor_add(
                fsl(Lb, c, hoff, 1024), G16a[:], fsl(Xb, c, hoff, 1024)))
    if debug:
        wait("se", "ve", v_x2)
        dma("se", "so", dbg["d_L"], Lb[:])

    h_buf = Xb
    v_outdma = None
    last_out_pe = 0
    prev_out_ve = 0

    def emit_ln3(tau):
        mark("LN3t%d" % tau)
        toff = 512 * tau
        pe_s3 = 0
        for cb in range(4):
            wait("ve", "pe", pe_s3)
            if cb == 0:
                wait("ve", "ve", v_x2)
            gsrc = bass.AP(Lb.tensor, 2048 * 4 * cb + toff,
                           [[32768, 128], [2048, 4], [1, 512]])
            gdst = bass.AP(GSQ.tensor, 0, [[2048, 128], [512, 4], [1, 512]])
            vsq = vop(lambda E, a=gdst, b=gsrc: E.tensor_mul(a, b, b))
            wait("pe", "ve", vsq)
            aux_guard()
            fns = [lambda E: E.ldweights(onesb[:])]
            for j in range(4):
                c = 4 * cb + j
                fns.append(lambda E, c=c, toff=toff: E.matmul(
                    pR[0:1, 0:512], onesb[:], fsl(Lb, c, toff, 512),
                    start=(c == 0), stop=(c == 15)))
                fns.append(lambda E, j=j, c=c: E.matmul(
                    pR[0:1, 512:1024], onesb[:], GSQ[:, 512 * j:512 * (j + 1)],
                    start=(c == 0), stop=(c == 15)))
            pe_s3 = peg(fns)
        wait("ve", "pe", pe_s3)
        vop(lambda E: E.tensor_copy(s34row[:], pR[0:1, 0:512]))
        vop(lambda E: E.tensor_copy(q34row[:], pR[0:1, 512:1024]))
        vop(lambda E: E.tensor_scalar_mul(s34row[:], s34row[:], 1.0 / D))
        vop(lambda E: E.tensor_scalar_mul(q34row[:], q34row[:], 1.0 / D))
        vop(lambda E: E.tensor_mul(pR[0:1, 0:512], s34row[:], s34row[:]))
        vop(lambda E: E.tensor_sub(q34row[:], q34row[:], pR[0:1, 0:512]))
        v_v3 = vop(lambda E: E.tensor_scalar_add(q34row[:], q34row[:], EPS))
        wait("se", "ve", v_v3)
        s_r3 = sop(lambda E: E.activation(q34row[:], q34row[:], AF.Sqrt))
        wait("ve", "se", s_r3)
        v_r3 = vop(lambda E: E.reciprocal(q34row[:], q34row[:]))
        v_mr3 = vop(lambda E: E.tensor_mul(s34row[:], s34row[:], q34row[:]))
        v_b1 = bcast(q34row, r3b, 512, ("ve", v_r3))
        v_b2 = bcast(s34row, m3r3b, 512, ("ve", v_mr3))
        wait("ve", "ve", max(v_b1, v_b2))
        v_ln3 = 0
        for c in range(16):
            vop(lambda E, c=c, toff=toff: E.tensor_mul(
                Gf5a[:], fsl(Lb, c, toff, 512), r3b[:]))
            v_ln3 = vop(lambda E, c=c: E.tensor_sub(
                ln3f[:, 512 * c:512 * (c + 1)], Gf5a[:], m3r3b[:]))
        return v_ln3

    v_ln3 = emit_ln3(0)
    for tau in range(4):
        toff = 512 * tau
        mark("C1t%d" % tau)
        pe_snapshot = cnt["pe"]
        pe_g = [0] * 64
        pe_u = [0] * 64
        h_ev = [0] * 64
        h_sil = [0] * 64
        for b in range(32):
            slot = b % 2
            wait("gp", "pe", pe_u[2 * (b - 2) + 1] if b >= 2 else pe_snapshot)
            gv = dma("gp", "wb%d" % slot, GB[:, slot], wg_h[:, 2 * b:2 * b + 2])
            uv = dma("gp", "wu%d" % slot, UB[:, slot], wu_h[:, 2 * b:2 * b + 2])
            waitt("pe", gv)
            waitt("pe", uv)
            for j in range(2):
                ic = 2 * b + j
                psl = ic % 2
                if ic >= 2:
                    wait("pe", "se", h_sil[ic - 2])
                    wait("pe", "ve", h_ev[ic - 2])
                if ic == 0:
                    wait("pe", "ve", v_ln3)
                gfns = []
                for hc in range(16):
                    gfns.append(lambda E, hc=hc, sl=slot, j=j: E.ldweights(
                        GB[:, sl, j, hc, :]))
                    gfns.append(lambda E, hc=hc, sl=slot, j=j, p=psl: E.matmul(
                        pM0[:, 512 * p:512 * (p + 1)], GB[:, sl, j, hc, :],
                        ln3f[:, 512 * hc:512 * (hc + 1)],
                        start=(hc == 0), stop=(hc == 15)))
                pe_g[ic] = peg(gfns)
                ufns = []
                for hc in range(16):
                    ufns.append(lambda E, hc=hc, sl=slot, j=j: E.ldweights(
                        UB[:, sl, j, hc, :]))
                    ufns.append(lambda E, hc=hc, sl=slot, j=j, p=psl: E.matmul(
                        pM1[:, 512 * p:512 * (p + 1)], UB[:, sl, j, hc, :],
                        ln3f[:, 512 * hc:512 * (hc + 1)],
                        start=(hc == 0), stop=(hc == 15)))
                pe_u[ic] = peg(ufns)
                wait("se", "pe", pe_g[ic])
                h_sil[ic] = sop(lambda E, ic=ic, p=psl: E.activation(
                    Gf5a[:] if p == 0 else Gf5b[:], pM0[:, 512 * p:512 * (p + 1)],
                    AF.Silu, scale=gsc[:, ic:ic + 1]))
                wait("ve", "pe", pe_u[ic])
                wait("ve", "se", h_sil[ic])
                if ic == 48 and v_outdma:
                    waitt("ve", v_outdma)
                h_ev[ic] = vop(lambda E, ic=ic, p=psl: E.scalar_tensor_tensor(
                    h_buf[:, 512 * ic:512 * (ic + 1)], pM1[:, 512 * p:512 * (p + 1)],
                    usc[:, ic:ic + 1], Gf5a[:] if p == 0 else Gf5b[:],
                    ALU.mult, ALU.mult))
        v_hdone = h_ev[63]
        wait("pe", "ve", v_hdone)
        aux_guard()
        sh_fns = []
        for ic in range(64):
            sh_fns.append(lambda E, ic=ic: E.ldweights(onesb[:]))
            sh_fns.append(lambda E, ic=ic: E.matmul(
                pR[0:1, 0:512], onesb[:], h_buf[:, 512 * ic:512 * (ic + 1)],
                start=(ic == 0), stop=(ic == 63)))
        pe_sh = peg(sh_fns)
        wait("ve", "pe", pe_sh)
        vop(lambda E: E.tensor_copy(shrow[:], pR[0:1, 0:512]))
        v_shneg = vop(lambda E: E.tensor_scalar_mul(shrow[:], shrow[:], -128.0))
        if debug and tau == 0:
            wait("se", "ve", v_hdone)
            dma("se", "so", dbg["d_ln3"], ln3f[:])
            dma("se", "so", dbg["d_h"], h_buf[:])
        mark("C2t%d" % tau)
        pe_s4 = 0
        dn_ev = [0] * 16
        dn_sil = [0] * 16
        dn_hist = [pe_u[63], pe_u[63]]
        for qq in range(4):
            pe_blk = [0] * 4
            for b in range(4):
                slot = b % 2
                wait("gp", "pe", dn_hist[slot])
                wv = dma("gp", "wd%d" % slot, DNB[:, slot],
                         wd_h[:, 16 * b:16 * b + 16, qq // 2,
                              4 * (qq % 2):4 * (qq % 2) + 4, :])
                waitt("pe", wv)
                if b == 0:
                    if qq == 0:
                        wait("pe", "ve", v_hdone)
                        wait("pe", "se", h_sil[63])
                    else:
                        wait("pe", "se", dn_sil[4 * qq - 1])
                dfns = []
                for f in range(4):
                    dst = pM0 if f < 2 else pM1
                    for il in range(16):
                        ic = 16 * b + il
                        dfns.append(lambda E, il=il, f=f, sl=slot: E.ldweights(
                            DNB[:, sl, il, f, :]))
                        dfns.append(lambda E, il=il, f=f, sl=slot, d=dst, ic=ic: E.matmul(
                            d[:, 512 * (f % 2):512 * (f % 2 + 1)], DNB[:, sl, il, f, :],
                            h_buf[:, 512 * ic:512 * (ic + 1)],
                            start=(ic == 0), stop=False))
                pe_blk[b] = peg(dfns)
                dn_hist[slot] = pe_blk[b]
            wait("pe", "ve", v_shneg)
            cfns = []
            for f in range(4):
                dst = pM0 if f < 2 else pM1
                cfns.append(lambda E, f=f, d=dst: E.matmul(
                    d[:, 512 * (f % 2):512 * (f % 2 + 1)], onesf[:], shrow[:],
                    start=False, stop=True))
            pe_corr = peg(cfns)
            for f in range(4):
                hf = 4 * qq + f
                src = pM0 if f < 2 else pM1
                wait("se", "pe", pe_corr)
                if hf == 0 and prev_out_ve:
                    wait("se", "ve", prev_out_ve)
                dn_sil[hf] = sop(lambda E, hf=hf, src=src, f=f: E.activation(
                    ffnf[:, 512 * hf:512 * (hf + 1)],
                    src[:, 512 * (f % 2):512 * (f % 2 + 1)],
                    AF.Identity, scale=dsc[:, hf:hf + 1]))
        pe_s4 = 0
        for cb in range(4):
            wait("ve", "pe", pe_s4)
            if cb == 0:
                wait("ve", "se", dn_sil[15])
            gsrc = bass.AP(ffnf.tensor, 2048 * cb, [[8192, 128], [512, 4], [1, 512]])
            gdst = bass.AP(GSQ.tensor, 0, [[2048, 128], [512, 4], [1, 512]])
            vsq = vop(lambda E, a=gdst, b=gsrc: E.tensor_mul(a, b, b))
            wait("pe", "ve", vsq)
            aux_guard()
            fns = [lambda E: E.ldweights(onesb[:])]
            for j in range(4):
                hf = 4 * cb + j
                fns.append(lambda E, hf=hf: E.matmul(
                    pR[0:1, 0:512], onesb[:], ffnf[:, 512 * hf:512 * (hf + 1)],
                    start=(hf == 0), stop=(hf == 15)))
                fns.append(lambda E, j=j, hf=hf: E.matmul(
                    pT[0:1, 0:512], onesb[:], GSQ[:, 512 * j:512 * (j + 1)],
                    start=(hf == 0), stop=(hf == 15)))
            pe_s4 = peg(fns)
        if debug and tau == 0:
            wait("se", "ve", dn_ev[15])
            dma("se", "so", dbg["d_ffn"], ffnf[:])
        mark("OUTt%d" % tau)
        wait("ve", "pe", pe_s4)
        if True:
            pass
        vop(lambda E: E.tensor_copy(s34row[:], pR[0:1, 0:512]))
        vop(lambda E: E.tensor_copy(q34row[:], pT[0:1, 0:512]))
        vop(lambda E: E.tensor_scalar_mul(s34row[:], s34row[:], 1.0 / D))
        vop(lambda E: E.tensor_scalar_mul(q34row[:], q34row[:], 1.0 / D))
        vop(lambda E: E.tensor_mul(pT[0:1, 0:512], s34row[:], s34row[:]))
        vop(lambda E: E.tensor_sub(q34row[:], q34row[:], pT[0:1, 0:512]))
        v_v4 = vop(lambda E: E.tensor_scalar_add(q34row[:], q34row[:], EPS))
        wait("se", "ve", v_v4)
        s_r4 = sop(lambda E: E.activation(q34row[:], q34row[:], AF.Sqrt))
        wait("ve", "se", s_r4)
        v_r4 = vop(lambda E: E.reciprocal(q34row[:], q34row[:]))
        v_mr4 = vop(lambda E: E.tensor_mul(s34row[:], s34row[:], q34row[:]))
        v_b1 = bcast(q34row, r4b, 512, ("ve", v_r4))
        v_b2 = bcast(s34row, m4r4b, 512, ("ve", v_mr4))
        v_b12 = max(v_b1, v_b2)
        if tau < 3:
            v_ln3 = emit_ln3(tau + 1)
        wait("ve", "ve", v_b12)
        for fl in range(2):
            if v_outdma:
                waitt("ve", v_outdma)
            ev_slot = [0, 0, 0, 0]
            for c in range(16):
                co = 256 * fl
                vop(lambda E, c=c, co=co: E.tensor_mul(
                    G16a[:, 0:256], ffnf[:, 512 * c + co:512 * c + co + 256],
                    r4b[:, co:co + 256]))
                vop(lambda E, co=co: E.tensor_sub(
                    G16a[:, 0:256], G16a[:, 0:256], m4r4b[:, co:co + 256]))
                v_of = vop(lambda E, c=c, co=co, toff=toff: E.tensor_add(
                    Gf5b[:, 0:256], G16a[:, 0:256], fsl(Lb, c, toff + co, 256)))
                sl = c % 4
                wait("pe", "ve", v_of)
                if sl == 0 and c < 4:
                    aux_guard()
                else:
                    wait("pe", "ve", ev_slot[sl])
                fns = []
                for k in range(2):
                    fns.append(lambda E, sl=sl, k=k: E.transpose(
                        pT[:, 256 * sl + 128 * k:256 * sl + 128 * (k + 1)],
                        Gf5b[:, 128 * k:128 * (k + 1)], ident))
                ptv = peg(fns)
                last_out_pe = ptv
                wait("ve", "pe", ptv)
                dst = bass.AP(outTok.tensor, 128 * c, [[4096, 128], [2048, 2], [1, 128]])
                src = bass.AP(pT.tensor, 256 * sl, [[1024, 128], [128, 2], [1, 128]])
                ev_slot[sl] = vop(lambda E, d=dst, s=src: E.tensor_copy(d, s))
            wait("se", "ve", max(ev_slot))
            prev_out_ve = max(ev_slot)
            ydst = bass.AP(y_h.tensor, (toff + 256 * fl) * 2048,
                           [[2048, 128], [2048 * 128, 2], [1, 2048]])
            v_outdma = dma("se", "so", ydst, outTok[:])
    wait("se", "so", cnt["so"])

    def run_stream(E, eng):
        lst = q[eng]
        bounds = [(nm, idx[eng]) for nm, idx in marks] + [("_end", len(lst))]
        pos = 0
        seg = 0
        cur = "P0"
        while pos < len(lst):
            while seg < len(bounds) and bounds[seg][1] <= pos:
                cur = bounds[seg][0]
                seg += 1
            nxt = bounds[seg][1] if seg < len(bounds) else len(lst)
            if nxt == pos:
                seg += 1
                continue
            with nc.named_scope(cur):
                for f in lst[pos:nxt]:
                    f(E)
            pos = nxt

    with nc.Block() as block:
        @block.sync
        def _(E):
            run_stream(E, "sy")

        @block.gpsimd
        def _(E):
            run_stream(E, "gp")

        @block.tensor
        def _(E):
            run_stream(E, "pe")

        @block.vector
        def _(E):
            run_stream(E, "ve")

        @block.scalar
        def _(E):
            run_stream(E, "se")

    es.close()
    return nc


def _bfarr(a):
    return np.ascontiguousarray(a.astype(ml_dtypes.bfloat16))


def _tile_w(wt):
    K, M = wt.shape
    return np.ascontiguousarray(
        wt.reshape(K // 128, 128, M // 128, 128).transpose(1, 2, 0, 3))


def _cols(vec):
    return np.ascontiguousarray(np.asarray(vec).reshape(-1, 128).T)


def make_inputs(L0_attn_X, L0_attn_W_in, L0_attn_b_in, L0_attn_W_out, L0_attn_b_out,
                L0_ffn_gate_w, L0_ffn_gate_scale, L0_ffn_up_w, L0_ffn_up_scale,
                L0_ffn_down_w, L0_ffn_down_scale):
    X = np.ascontiguousarray(np.asarray(L0_attn_X, dtype=np.float32)).reshape(16384, 2048)
    W_in = np.asarray(L0_attn_W_in, dtype=np.float32)
    b_in = np.asarray(L0_attn_b_in, dtype=np.float32)
    W_out = np.asarray(L0_attn_W_out, dtype=np.float32)
    b_out = np.asarray(L0_attn_b_out, dtype=np.float32)
    gw = np.asarray(L0_ffn_gate_w, dtype=np.int32)
    uw = np.asarray(L0_ffn_up_w, dtype=np.int32)
    dw = np.asarray(L0_ffn_down_w, dtype=np.int32)
    gsv = np.asarray(L0_ffn_gate_scale, dtype=np.float32)
    usv = np.asarray(L0_ffn_up_scale, dtype=np.float32)
    dsv = np.asarray(L0_ffn_down_scale, dtype=np.float32)

    rot = 1024
    perm = np.concatenate([np.arange(0, rot, 2), np.arange(1, rot, 2),
                           np.arange(rot, 2048)])
    colmap = np.concatenate([perm, 2048 + perm, 4096 + perm])

    wqkv = _tile_w(np.ascontiguousarray(W_in.T[:, colmap]))
    bqkv = _cols(b_in[colmap])
    woutT = np.ascontiguousarray(W_out.T[perm, :])
    wout = _tile_w(woutT)
    bov = _cols(b_out)
    wg = _tile_w(np.ascontiguousarray(gw.T))
    wu = _tile_w(np.ascontiguousarray(uw.T))
    wdT = np.ascontiguousarray(dw.T)
    wd = np.ascontiguousarray(
        wdT.reshape(64, 128, 2, 8, 128).transpose(1, 0, 2, 3, 4))
    gsc = _cols(gsv)
    usc = _cols(usv)
    dsc = _cols(dsv)
    ident = np.eye(128, dtype=np.float32)

    inv_freq = (1.0 / (10000.0 ** (np.arange(0, rot, 2, dtype=np.float64) / rot)))

    in_maps = []
    for core in range(8):
        p0 = (core % 2) * 2048
        pos = np.arange(p0, p0 + 2048, dtype=np.float64)
        ang = pos[None, :] * inv_freq[:, None]
        cosc = _bfarr(np.cos(ang).reshape(4, 128, 2048).transpose(1, 0, 2))
        sinc = _bfarr(np.sin(ang).reshape(4, 128, 2048).transpose(1, 0, 2))
        in_maps.append({
            "x": np.ascontiguousarray(X[2048 * core:2048 * (core + 1)]),
            "wqkv": wqkv, "wout": wout, "wg": wg, "wu": wu, "wd": wd,
            "bqkv": bqkv, "bo": bov, "gs": gsc, "us": usc, "ds": dsc,
            "cost": cosc, "sint": sinc, "idn": ident,
        })
    return in_maps


def kernel(**inputs):
    in_maps = make_inputs(**inputs)
    nc = build_program(debug=False)
    res = run_bass_kernel_spmd(nc, in_maps, core_ids=list(range(8)))
    out = np.concatenate([np.asarray(r["y"]) for r in res.results], axis=0)
    return out.reshape(4, 4096, 2048)
